# version 21
# baseline (speedup 1.0000x reference)
import numpy as np
import ml_dtypes
from contextlib import ExitStack

import concourse.bass as bass
import concourse.mybir as mybir
from concourse.bass_utils import run_bass_kernel_spmd

F32 = mybir.dt.float32
BF16 = mybir.dt.bfloat16
I32 = mybir.dt.int32
AF = mybir.ActivationFunctionType
ALU = mybir.AluOpType
AX = mybir.AxisListType

S = 2048
D = 1024
NT = 16
NCH = 8
NP = 8
NKB = 16
DQ = 512
L0_SCALE = float(1.0 / np.sqrt(np.float64(16 + 512 + 512)))
L1_SCALE = 0.125
BF = ml_dtypes.bfloat16

import os
BUILD = int(os.environ.get("KBUILD", "4"))


class Sems:
    def __init__(self, ctx, nc, names):
        self.h = {}
        self.c = {}
        for n in names:
            self.h[n] = ctx.enter_context(nc.semaphore(n))
            self.c[n] = 0

    def reserve(self, name, n=1):
        self.c[name] += n
        return self.c[name]


class Res:

    def __init__(self, sems):
        self.sems = sems
        self.w = {}
        self.r = {}
        self.waited = {}

    def _wait(self, pend, ename, sname, tok):
        if sname == ename:
            return
        wd = self.waited.setdefault(ename, {})
        if wd.get(sname, 0) >= tok:
            return
        pend.append((sname, tok))
        wd[sname] = tok

    def rd(self, pend, ename, *regions):
        for reg in regions:
            if reg in self.w:
                s, t = self.w[reg]
                self._wait(pend, ename, s, t)

    def wr(self, pend, ename, *regions):
        for reg in regions:
            if reg in self.w:
                s, t = self.w[reg]
                self._wait(pend, ename, s, t)
            for s, t in self.r.get(reg, {}).items():
                self._wait(pend, ename, s, t)

    def did_rd(self, ename, tok, *regions):
        for reg in regions:
            self.r.setdefault(reg, {})[ename] = tok

    def did_wr(self, ename, tok, *regions):
        for reg in regions:
            self.w[reg] = (ename, tok)
            self.r[reg] = {}


def build_program():
    nc = bass.Bass("TRN2", target_bir_lowering=False, debug=False)

    def din(name, shape, dt=F32):
        return nc.dram_tensor(name, list(shape), dt, kind="ExternalInput").ap()

    x_in = din("x", (S, D))
    out_d = nc.dram_tensor("out", [S, D], F32, kind="ExternalOutput").ap()

    wdq_d = din("wdq", (D, DQ))
    wdkv_d = din("wdkv", (D, DQ))
    wuq_d = din("wuq", (DQ, D))
    wqr_d = din("wqr", (DQ, D))
    wuk_d = din("wuk", (DQ, D))
    wkr_d = din("wkr", (D, D))
    wuv_d = din("wuv", (DQ, D))
    wo0_d = din("wo0", (D, D))
    bdq_d = din("bdq_t", (128, 4))
    bdkv_d = din("bdkv_t", (128, 4))
    brow0_d = din("brow0", (1, 5 * D))
    bwo0_d = din("bwo0", (1, D))
    rts_d = din("rts", (NP, 128, 2048), BF16)
    rc0_d = din("rc0", (33, 2048))
    swp_d = din("swp", (128, 128))
    bqrc_d = din("bqr_c", (128, 8))
    buqc_d = din("buq_c", (128, 8))
    bkrc_d = din("bkr_c", (128, 8))
    bukc_d = din("buk_c", (128, 8))

    tw_d = din("tw", (D, D), I32)
    fbr_d = din("fb_rep", (128, D))
    asc_d = din("ascale", (1, 1))

    wqkv_d = din("wqkv", (D, 3 * D))
    bqk_d = din("bqk_row", (1, 2 * D))
    bv1_d = din("bv1", (1, D))
    wo1_d = din("wo1", (D, D))
    bwo1_d = din("bwo1", (1, D))

    win_d = din("win", (D, 64))
    binc_d = din("bin_c", (64, 1))
    w1c_d = din("w1c", (128, 8))
    b1c_d = din("b1c", (128, 8))
    emat_d = din("emat", (64, D))
    e2_d = din("e2", (D, 64))
    w2c_d = din("w2c", (128, 8))
    browf_d = din("browf", (1, D + 64))
    wout1_d = din("wout1", (64, D))

    ident_d = din("ident", (128, 128))
    cmask_d = din("cmask", (128, 128))

    ctx = ExitStack()
    with ctx:
        E = ctx.enter_context
        X = E(nc.sbuf_tensor("X", [128, NT * D], F32))
        XNT = E(nc.sbuf_tensor("XNT", [128, NCH * S], BF16))
        CTQ = E(nc.sbuf_tensor("CTQ", [128, 8192], BF16))
        CTK = E(nc.sbuf_tensor("CTK", [128, 8192], BF16))
        AO = E(nc.sbuf_tensor("AO", [128, 8192], BF16))
        RB = E(nc.sbuf_tensor("RB", [128, 2048], BF16))
        CT = E(nc.sbuf_tensor("CT", [128, 2048], BF16))
        QF = E(nc.sbuf_tensor("QF", [128, 2048], BF16))
        KF = E(nc.sbuf_tensor("KF", [128, 2048], BF16))
        VP = E(nc.sbuf_tensor("VP", [128, 2080], BF16))
        PB = E(nc.sbuf_tensor("PB", [128, 4096], BF16))
        WSL = E(nc.sbuf_tensor("WSL", [128, 6144], BF16))
        RTS = E(nc.sbuf_tensor("RTS", [128, 2048], BF16))
        SWP = E(nc.sbuf_tensor("SWP", [128, 128], BF16))
        BROW = E(nc.sbuf_tensor("BROW", [1, 6144], BF16))
        F1 = E(nc.sbuf_tensor("F1", [128, 1024], F32))
        STAT = E(nc.sbuf_tensor("STAT", [128, 32], F32))
        ST2 = E(nc.sbuf_tensor("ST2", [128, 272], F32))
        REP = E(nc.sbuf_tensor("REP", [128, 96], F32))
        MB = E(nc.sbuf_tensor("MB", [128, 2], BF16))
        IDE = E(nc.sbuf_tensor("IDE", [128, 128], BF16))
        CMK = E(nc.sbuf_tensor("CMK", [128, 128], BF16))
        ONER = E(nc.sbuf_tensor("ONER", [1, 512], BF16))
        ONEC = E(nc.sbuf_tensor("ONEC", [128, 1], BF16))
        ONERF = E(nc.sbuf_tensor("ONERF", [65, 64], F32))
        CONSTF = E(nc.sbuf_tensor("CONSTF", [128, 2], F32))

        PS = E(nc.psum_tensor("PS", [128, 2048], F32))
        PVA = E(nc.psum_tensor("PVA", [65, 1024], F32))
        PVB = E(nc.psum_tensor("PVB", [65, 1024], F32))

        sem = Sems(ctx, nc, ["T", "V", "Sc", "D1", "D2", "G"])
        res = Res(sem)
        blk = E(nc.Block())

        def T(f):
            blk.tensor(f)

        def V(f):
            blk.vector(f)

        def SC(f):
            blk.scalar(f)

        def D1(f):
            blk.sync(f)

        def D2(f):
            blk.gpsimd(f)

        B_QR, B_UQ, B_KR, B_UK, B_UV = 0, 1024, 2048, 3072, 4096
        B_WO = 5120
        B_V1, B_FO, B_B2 = 0, 1024, 2048
        B_BQK1 = 2048
        R_BA, R_SREP, R_RCP = 0, 1, 2
        R_BDQ, R_BDKV = 4, 8
        R_W1C, R_B1C, R_W2C = 32, 40, 48
        R_BQR, R_BUQ, R_BKR, R_BUK = 64, 72, 80, 88

        PSa = PS[:, 0:1024]
        PSb = PS[:, 1024:2048]

        def wsl(h, off, n):
            return WSL[:, h * 3072 + off:h * 3072 + off + n]

        def evac(preg, dreg, dst, src, use_sc):
            if use_sc:
                scgrp((preg,), (dreg,),
                      lambda sc, dst=dst, src=src: sc.copy(dst, src))
            else:
                vgrp((preg,), (dreg,),
                     lambda v, dst=dst, src=src: v.tensor_copy(dst, src))

        STREAMS = {"T": [], "V": [], "Sc": [], "D1": [], "D2": []}
        ORDER = []

        _EXP = {"PSa": ("PSa0", "PSa1"), "PSb": ("PSb0", "PSb1")}

        def _expand(regions):
            out = []
            for r in regions:
                out.extend(_EXP.get(r, (r,)))
            return out

        def _grp(stream, ename, sname, reads, writes, body, drain=False,
                 inc_n=1):
            reads = _expand(reads)
            writes = _expand(writes)
            pend = []
            res.rd(pend, ename, *reads)
            res.wr(pend, ename, *writes)
            tok = sem.reserve(sname, inc_n)
            STREAMS[stream].append((pend, body, [(sname, inc_n)], drain))
            ORDER.append((stream, STREAMS[stream][-1]))
            res.did_rd(ename, tok, *reads)
            res.did_wr(ename, tok, *writes)

        def tgrp(reads, writes, body):
            _grp("T", "T", "T", reads, writes, body)

        def vgrp(reads, writes, body, drain=False):
            _grp("V", "V", "V", reads, writes, body, drain)

        def scgrp(reads, writes, body, drain=False):
            _grp("Sc", "Sc", "Sc", reads, writes, body, drain)

        def ggrp(reads, writes, body):
            _grp("D2", "G", "G", reads, writes, body)

        def dma(q, dst, src, writes, reads=()):
            _grp(q, q, q, reads, writes,
                 lambda d, dst=dst, src=src: d.dma_start(dst, src),
                 inc_n=16)

        dbg_on = bool(os.environ.get("KDBG"))
        dbg_names = []

        def dbg(name, ap, shape, reads):
            if not dbg_on:
                return
            dt = nc.dram_tensor("dbg_" + name, list(shape),
                                ap.tensor.dtype,
                                kind="ExternalOutput").ap()
            dbg_names.append(name)
            pend = []
            res.rd(pend, "D1", *reads)
            tok = sem.reserve("D1", 16)
            STREAMS["D1"].append((pend,
                                  lambda d, dt=dt, ap=ap: d.dma_start(dt, ap),
                                  [("D1", 16)], False))
            res.did_rd("D1", tok, *reads)

        def flush_streams():
            def run(eng, lst):
                for waits, body, incs, drain in lst:
                    for s, t in waits:
                        eng.wait_ge(sem.h[s], t)
                    last = body(eng)
                    if drain:
                        last = eng.drain()
                    for s, n in incs:
                        last.then_inc(sem.h[s], n)
            sect = {"T": T, "V": V, "Sc": SC, "D1": D1, "D2": D2}
            if os.environ.get("KSECT") == "many":
                for stream, entry in ORDER:
                    sect[stream](lambda e, entry=entry: run(e, [entry]))
                tail = STREAMS["D1"][len([1 for s, _ in ORDER
                                          if s == "D1"]):]
                if tail:
                    D1(lambda d: run(d, tail))
                return
            T(lambda te: run(te, STREAMS["T"]))
            V(lambda v: run(v, STREAMS["V"]))
            SC(lambda sc: run(sc, STREAMS["Sc"]))
            D1(lambda d: run(d, STREAMS["D1"]))
            D2(lambda d: run(d, STREAMS["D2"]))

        dma("D1", X[:, :].rearrange("p (t d) -> p t d", t=NT),
            x_in.rearrange("(t p) d -> p t d", p=128), ("X",))
        dma("D2", IDE[:, :], ident_d, ("IDE",))
        dma("D2", CMK[:, :], cmask_d, ("CMK",))
        RC0 = F1[0:33, 0:1024].bitcast(BF16)
        dma("D2", RC0, rc0_d, ("F1",))
        dma("D2", SWP[:, :], swp_d, ("SWP",))
        dma("D2", REP[:, R_BQR:R_BQR + 8], bqrc_d, ("REP",))
        dma("D2", REP[:, R_BUQ:R_BUQ + 8], buqc_d, ("REP",))
        dma("D2", REP[:, R_BKR:R_BKR + 8], bkrc_d, ("REP",))
        dma("D2", REP[:, R_BUK:R_BUK + 8], bukc_d, ("REP",))
        dma("D2", BROW[:, 0:5 * D], brow0_d, ("BROW",))
        dma("D2", BROW[:, B_WO:B_WO + D], bwo0_d, ("BROW",))
        dma("D2", STAT[0:1, 8:9], asc_d, ("STATf",))
        dma("D2", REP[:, R_BDQ:R_BDQ + 4], bdq_d, ("REP",))
        dma("D2", REP[:, R_BDKV:R_BDKV + 4], bdkv_d, ("REP",))

        vgrp((), ("ONES", "VP", "CONSTF"), lambda v: (
            v.memset(ONER[:, :], 1.0),
            v.memset(ONEC[:, :], 1.0),
            v.memset(ONERF[:, :], 1.0),
            v.memset(CONSTF[:, 0:1], 1e-5),
            v.memset(CONSTF[:, 1:2], -1.0),
            v.memset(VP[:, :], 1.0))[-1])

        LN_MV, LN_STD, LN_RSTD, LN_NMR = 192, 224, 240, 256

        def emit_ln_stats_all():
            def vb(v):
                last = None
                for t in range(NT):
                    v.bn_stats(ST2[:, t * 12:t * 12 + 6],
                               X[:, t * D:t * D + 512])
                    last = v.bn_stats(ST2[:, t * 12 + 6:t * 12 + 12],
                                      X[:, t * D + 512:t * D + 1024])
                return last
            vgrp(("X",), ("BNS",), vb, drain=True)

            def vb2(v):
                last = None
                for t in range(NT):
                    last = v.bn_aggr(
                        ST2[:, LN_MV + 2 * t:LN_MV + 2 * t + 2],
                        ST2[:, t * 12:t * 12 + 12])
                return last
            vgrp(("BNS",), ("LNV",), vb2, drain=True)
            varv = ST2[:, LN_MV:LN_MV + 32].rearrange(
                "p (t k) -> p t k", k=2)[:, :, 1:2]
            stdv = ST2[:, LN_STD:LN_STD + 16].rearrange(
                "p (t k) -> p t k", k=1)
            scgrp(("LNV", "CONSTF"), ("LNV",), lambda sc: (
                sc.activation(stdv, varv, AF.Ln, bias=CONSTF[:, 0:1]),
                sc.drain(),
                sc.activation(ST2[:, LN_RSTD:LN_RSTD + 16],
                              ST2[:, LN_STD:LN_STD + 16], AF.Exp,
                              scale=-0.5))[-1])
            meanv = ST2[:, LN_MV:LN_MV + 32].rearrange(
                "p (t k) -> p t k", k=2)[:, :, 0:1]
            nmrv = ST2[:, LN_NMR:LN_NMR + 16].rearrange(
                "p (t k) -> p t k", k=1)
            vgrp(("LNV",), ("LNV",), lambda v: v.scalar_tensor_tensor(
                nmrv, meanv, -1.0,
                ST2[:, LN_RSTD:LN_RSTD + 16].rearrange(
                    "p (t k) -> p t k", k=1),
                ALU.mult, ALU.mult), drain=True)

        def ln_apply(t, dst, dreg):
            scgrp(("X", "LNV"), (dreg,), lambda sc: sc.activation(
                dst, X[:, t * D:(t + 1) * D], AF.Identity,
                bias=ST2[:, LN_NMR + t:LN_NMR + t + 1],
                scale=ST2[:, LN_RSTD + t:LN_RSTD + t + 1]))

        def emit_ln_to_xnt():
            emit_ln_stats_all()
            for t in range(NT):
                lreg = "RB%d" % (t % 2)
                lnb = RB[:, (t % 2) * D:(t % 2) * D + D]
                ln_apply(t, lnb, lreg)
                for half in range(2):
                    preg = "PSa" if half == 0 else "PSb"
                    pt = (PSa if half == 0 else PSb)[:, 0:256].bitcast(BF16)

                    def tb(te, lnb=lnb, half=half, pt=pt):
                        last = None
                        for cc in range(4):
                            c = half * 4 + cc
                            last = te.transpose(
                                pt[:, cc * 128:cc * 128 + 128],
                                lnb[:, c * 128:c * 128 + 128], IDE[:, :])
                        return last
                    tgrp((lreg, "IDE"), (preg,), tb)

                    def vb(v, t=t, half=half, pt=pt):
                        dst = XNT[:, :].rearrange(
                            "p (c s) -> p c s", c=NCH)[
                            :, half * 4:half * 4 + 4,
                            t * 128:t * 128 + 128]
                        return v.tensor_copy(
                            dst, pt.rearrange("p (c s) -> p c s", c=4))
                    vgrp((preg,), ("XNT",), vb)

        def emit_ct():
            dma("D2", AO[:, 0:4096].rearrange("p (c n) -> p c n", c=NCH),
                wdq_d.rearrange("(c p) n -> p c n", p=128), ("AO",))
            dma("D2", AO[:, 4096:8192].rearrange("p (c n) -> p c n", c=NCH),
                wdkv_d.rearrange("(c p) n -> p c n", p=128), ("AO",))
            for which in range(2):
                wof = 4096 * which
                dst = CTQ if which == 0 else CTK
                dreg = "CTQ" if which == 0 else "CTK"
                bcol = R_BDQ if which == 0 else R_BDKV
                for cs in range(4):
                    for fs in range(4):
                        half = (cs * 4 + fs) % 2
                        preg = "PSa" if half == 0 else "PSb"
                        pslc = (PSa if half == 0 else PSb)[:, 0:512]

                        def tb(te, wof=wof, cs=cs, fs=fs, pslc=pslc):
                            last = None
                            for dc in range(NCH):
                                last = te.matmul(
                                    pslc,
                                    AO[:, wof + dc * 512 + cs * 128:
                                       wof + dc * 512 + cs * 128 + 128],
                                    XNT[:, dc * 2048 + fs * 512:
                                        dc * 2048 + fs * 512 + 512],
                                    start=(dc == 0), stop=(dc == NCH - 1))
                            return last
                        tgrp(("AO", "XNT"), (preg,), tb)
                        scgrp((preg, "REP"), (dreg,),
                              lambda sc, dst=dst, cs=cs, fs=fs, pslc=pslc,
                              bcol=bcol: sc.activation(
                                  dst[:, cs * 2048 + fs * 512:
                                      cs * 2048 + fs * 512 + 512],
                                  pslc, AF.Identity,
                                  bias=REP[:, bcol + cs:bcol + cs + 1]))

        def load_l0_slices(c):
            h = c % 2
            wreg = "WSL%d" % h
            sl = (c * 128, (c + 1) * 128)
            dma("D2", wsl(h, 0, 512).rearrange("p (cc j) -> p cc j", cc=4),
                wqr_d.rearrange("(cc p) d -> p cc d", p=128)[:, :, sl[0]:sl[1]],
                (wreg,))
            dma("D2", wsl(h, 512, 512).rearrange("p (cc j) -> p cc j", cc=4),
                wuq_d.rearrange("(cc p) d -> p cc d", p=128)[:, :, sl[0]:sl[1]],
                (wreg,))
            dma("D2", wsl(h, 1024, 512).rearrange("p (cc j) -> p cc j", cc=4),
                wuk_d.rearrange("(cc p) d -> p cc d", p=128)[:, :, sl[0]:sl[1]],
                (wreg,))
            dma("D2", wsl(h, 1536, 512).rearrange("p (cc j) -> p cc j", cc=4),
                wuv_d.rearrange("(cc p) d -> p cc d", p=128)[:, :, sl[0]:sl[1]],
                (wreg,))
            dma("D2", wsl(h, 2048, 1024).rearrange("p (cc j) -> p cc j", cc=8),
                wkr_d.rearrange("(cc p) d -> p cc d", p=128)[:, :, sl[0]:sl[1]],
                (wreg,))
            dma("D2", RTS[:, :], rts_d[c], ("RTS",))

        def load_l1_slices(c):
            h = c % 2
            wreg = "WSL%d" % h
            wre = wqkv_d.rearrange("(dc p) n -> p dc n", p=128)
            for k, off in ((0, 0), (1, 1024), (2, 2048)):
                dma("D2", wsl(h, k * 1024, 1024).rearrange(
                    "p (dc j) -> p dc j", dc=8),
                    wre[:, :, off + c * 128:off + c * 128 + 128], (wreg,))

        def jit_proj_fm(c, wcol, nchunk, srcT, sreg, bcol, dst, dreg):
            h = c % 2
            wreg = "WSL%d" % h
            for fs in range(4):
                qreg = ("PSa0", "PSb0", "PSa1", "PSb1")[fs]
                pslc = (PSa[:, 0:512], PSb[:, 0:512],
                        PSa[:, 512:1024], PSb[:, 512:1024])[fs]

                def tb(te, fs=fs, pslc=pslc, h=h, wcol=wcol, nchunk=nchunk,
                       srcT=srcT):
                    last = None
                    for cc in range(nchunk):
                        last = te.matmul(
                            pslc,
                            wsl(h, wcol + cc * 128, 128),
                            srcT[:, cc * 2048 + fs * 512:
                                 cc * 2048 + fs * 512 + 512],
                            start=(cc == 0), stop=(cc == nchunk - 1))
                    return last
                tgrp((wreg, sreg), (qreg,), tb)
                if fs % 2 == 0:
                    scgrp((qreg, "REP"), (dreg,), lambda sc, fs=fs,
                          pslc=pslc, bcol=bcol, dst=dst: sc.activation(
                              dst[:, fs * 512:fs * 512 + 512], pslc,
                              AF.Identity,
                              bias=REP[:, bcol + c:bcol + c + 1]))
                else:
                    vgrp((qreg, "REP"), (dreg,), lambda v, fs=fs, pslc=pslc,
                         bcol=bcol, dst=dst: v.tensor_scalar(
                             dst[:, fs * 512:fs * 512 + 512], pslc,
                             REP[:, bcol + c:bcol + c + 1], None, ALU.add))


        def jit_v_l0(c):
            h = c % 2
            wreg = "WSL%d" % h
            for half in range(2):
                preg = "PSa" if half == 0 else "PSb"
                phalf = PSa if half == 0 else PSb

                def tb(te, half=half, phalf=phalf, h=h):
                    last = None
                    for tt in range(8):
                        t = half * 8 + tt
                        o = phalf[:, tt * 128:tt * 128 + 128]
                        last = te.matmul(
                            o, ONER[0:1, 0:128],
                            BROW[0:1, B_UV + c * 128:B_UV + c * 128 + 128],
                            start=True, stop=False)
                        for cc in range(4):
                            last = te.matmul(
                                o,
                                CTK[:, cc * 2048 + t * 128:
                                    cc * 2048 + t * 128 + 128],
                                wsl(h, 1536 + cc * 128, 128),
                                start=False, stop=(cc == 3))
                    return last
                tgrp(("CTK", wreg, "ONES", "BROW"), (preg,), tb)

                def vb(v, half=half, phalf=phalf):
                    dst = VP[:, :].rearrange(
                        "p (kb a u) -> p kb a u", kb=16, a=2)[
                        :, half * 8:half * 8 + 8, :, 0:64]
                    return v.tensor_copy(
                        dst, phalf[:, 0:1024].rearrange(
                            "p (t a s) -> p t a s", t=8, a=2))
                vgrp((preg,), ("VP",), vb)

        def rope_fm(c, dst, dreg):
            def tbs(te):
                last = None
                for fs in range(4):
                    last = te.matmul(PS[:, fs * 512:fs * 512 + 512],
                                     SWP[:, :],
                                     CT[:, fs * 512:fs * 512 + 512],
                                     start=True, stop=True)
                return last
            tgrp(("CTr", "SWP"), ("PSa", "PSb"), tbs)

            def vbr(v):
                v.tensor_tensor(RB[:, :], PS[:, :], RTS[:, :], ALU.mult)
                v.drain()
                v.tensor_tensor(dst, dst, CT[:, :], ALU.add)
                v.drain()
                return v.tensor_tensor(dst, dst, RB[:, :], ALU.add)
            vgrp(("PSa", "PSb", "RTS", "CTr", dreg), 
                 (dreg, "RB0", "RB1"), vbr)

            if c == 0:
                def vb0(v):
                    for r in (0, 32):
                        v.tensor_tensor(RB[r:r + 1, 0:2048],
                                        CT[r:r + 1, :],
                                        RC0[r:r + 1, :], ALU.mult)
                    v.drain()
                    last = None
                    for r in (0, 32):
                        last = v.tensor_tensor(dst[r:r + 1, :],
                                               dst[r:r + 1, :],
                                               RB[r:r + 1, 0:2048], ALU.add)
                    return last
                vgrp(("CTr", "F1", dreg), (dreg, "RB0", "RB1"), vb0)


        def nb_part(which, buf, breg):
            vgrp((breg,), ("PB0", "PB1"), lambda v, buf=buf:
                 v.tensor_tensor(PB[:, 0:2048], buf[:, :], buf[:, :],
                                 ALU.mult))
            ph = PSa if which == 0 else PSb
            pr = ("PSa0", "PSa1") if which == 0 else ("PSb0", "PSb1")

            def tb(te, ph=ph):
                last = None
                for s4 in range(4):
                    last = te.matmul(
                        ph[0:1, (s4 % 2) * 512:(s4 % 2) * 512 + 512],
                        ONEC[:, :],
                        PB[:, s4 * 512:s4 * 512 + 512],
                        start=(s4 < 2), stop=(s4 >= 2))
                return last
            tgrp(("PB0", "PB1", "ONES"), pr, tb)
            mcol = 17 + which
            vgrp(pr, ("STATn",), lambda v, ph=ph, mcol=mcol:
                 v.reduce_max(STAT[0:1, mcol:mcol + 1], ph[0:1, 0:1024],
                              axis=AX.X), drain=True)

        def nb_combine(scale):
            def vb(v):
                v.tensor_tensor(STAT[0:1, 21:22], STAT[0:1, 17:18],
                                STAT[0:1, 18:19], ALU.add)
                v.drain()
                v.tensor_scalar(STAT[0:1, 21:22], STAT[0:1, 21:22],
                                -0.5 * scale, None, ALU.mult)
                v.drain()
                return v.tensor_copy(MB[0:1, 1:2], STAT[0:1, 21:22])
            vgrp(("STATn",), ("STATn", "MB"), vb)
            tgrp(("MB", "ONES"), ("PSa0",), lambda te:
                 te.matmul(PSa[:, 40:41], ONER[0:1, 0:128], MB[0:1, 1:2],
                           start=True, stop=True))
            vgrp(("PSa0",), ("REP",), lambda v:
                 v.tensor_copy(REP[:, R_BA:R_BA + 1], PSa[:, 40:41]))

        pb_ctr = [0]

        def emit_attention(c, layer):
            scale = L0_SCALE if layer == 0 else L1_SCALE
            causal = (layer == 1)
            aoc = (c % 4) * 2048
            nb_combine(scale)
            for qh in range(2):
                q0 = qh * 1024
                kb_list = list(range(NKB)) if not causal else \
                    list(range(0, qh * 8 + 8))
                subs = []
                for j, kb in enumerate(kb_list):
                    lo = 0 if not causal else max(0, kb * 128 - q0)
                    for qs in range(2):
                        a0 = max(lo, qs * 512)
                        b0 = (qs + 1) * 512
                        if a0 >= b0:
                            continue
                        msk = (causal and kb * 128 >= q0 and
                               qs * 512 <= lo < (qs + 1) * 512)
                        subs.append([kb, qs, a0, b0, j == 0,
                                     j == len(kb_list) - 1, msk, lo])

                def emit_qk_exp(sub):
                    kb, qs, a0, b0, first, last_kb, msk, lo = sub
                    pb = pb_ctr[0] % 4
                    pb_ctr[0] += 1
                    sub.append(pb)
                    pbreg = "PB%d" % pb
                    psq = ("PSa%d" % qs, "PSb%d" % qs)

                    def tb(te, kb=kb, a0=a0, b0=b0, q0=q0):
                        last = None
                        for (rb, psx) in ((0, PSa), (64, PSb)):
                            last = te.matmul(
                                psx[:, a0:b0],
                                KF[rb:rb + 64, kb * 128:kb * 128 + 128],
                                QF[rb:rb + 64, q0 + a0:q0 + b0],
                                start=True, stop=True,
                                tile_position=(rb, 0))
                        return last
                    tgrp(("QF", "KF"), psq, tb)

                    def scb(sc, a0=a0, b0=b0, pb=pb, qs=qs):
                        i_ap = PS[:, :].rearrange(
                            "p (h q) -> p h q", h=2)[:, :, a0:b0]
                        o_ap = PB[:, pb * 1024:pb * 1024 + 1024].rearrange(
                            "p (h q) -> p h q", h=2)[
                            :, :, a0 - qs * 512:b0 - qs * 512]
                        return sc.activation(o_ap, i_ap, AF.Exp,
                                             bias=REP[:, R_BA:R_BA + 1],
                                             scale=scale)
                    scgrp(psq + ("REP",), (pbreg,), scb)

                    if msk:
                        def vb(v, lo=lo, pb=pb, qs=qs):
                            ll = lo - qs * 512
                            pA = PB[:, pb * 1024 + ll:pb * 1024 + ll + 128]
                            pB = PB[:, pb * 1024 + 512 + ll:
                                    pb * 1024 + 512 + ll + 128]
                            v.tensor_tensor(pA, pA, CMK[:, :], ALU.mult)
                            return v.tensor_tensor(pB, pB, CMK[:, :],
                                                   ALU.mult)
                        vgrp((pbreg, "CMK"), (pbreg,), vb)

                def emit_pv(sub):
                    kb, qs, a0, b0, first, last_kb, msk, lo, pb = sub
                    pbreg = "PB%d" % pb

                    def tb2(te, kb=kb, a0=a0, b0=b0, pb=pb, qs=qs,
                            first=first, last_kb=last_kb):
                        last = None
                        for hh, pv in ((0, PVA), (1, PVB)):
                            last = te.matmul(
                                pv[:, a0:b0],
                                VP[:, kb * 130 + hh * 65:
                                   kb * 130 + hh * 65 + 65],
                                PB[:, pb * 1024 + hh * 512 + a0 - qs * 512:
                                   pb * 1024 + hh * 512 + b0 - qs * 512],
                                start=first, stop=last_kb,
                                skip_group_check=True)
                        return last
                    tgrp((pbreg, "VP"), ("PVA", "PVB"), tb2)

                for i, sub in enumerate(subs):
                    emit_qk_exp(sub)
                    if i >= 2:
                        emit_pv(subs[i - 2])
                emit_pv(subs[-2])
                emit_pv(subs[-1])

                rdA = F1[0:1, 0:1024]
                rdB = F1[64:65, 0:1024]
                PBF = PB[:, :].bitcast(F32)

                def scl(sc, rdA=rdA, rdB=rdB):
                    sc.activation(rdA, PVA[64:65, 0:1024], AF.Ln)
                    return sc.activation(rdB, PVB[64:65, 0:1024], AF.Ln)
                scgrp(("PVA", "PVB"), ("F1",), scl)

                def tb3(te, rdA=rdA, rdB=rdB):
                    last = None
                    for fs in range(2):
                        last = te.matmul(
                            PSa[0:64, fs * 512:fs * 512 + 512],
                            ONERF[0:1, :],
                            rdA[:, fs * 512:fs * 512 + 512],
                            start=True, stop=True)
                        last = te.matmul(
                            PSb[0:64, fs * 512:fs * 512 + 512],
                            ONERF[64:65, :],
                            rdB[:, fs * 512:fs * 512 + 512],
                            start=True, stop=True)
                    return last
                tgrp(("F1", "ONES"), ("PSa", "PSb"), tb3)

                scgrp(("PSa", "PSb"), ("PB0", "PB1"), lambda sc, PBF=PBF: (
                    sc.activation(PBF[0:64, 0:1024], PSa[0:64, 0:1024],
                                  AF.Exp, scale=-1.0),
                    sc.activation(PBF[0:64, 1024:2048], PSb[0:64, 0:1024],
                                  AF.Exp, scale=-1.0))[-1])

                def vb2(v, qh=qh, aoc=aoc, PBF=PBF):
                    v.tensor_tensor(AO[0:64, aoc + qh * 1024:
                                       aoc + qh * 1024 + 1024],
                                    PVA[0:64, 0:1024], PBF[0:64, 0:1024],
                                    ALU.mult)
                    return v.tensor_tensor(
                        CT[0:64, qh * 1024:qh * 1024 + 1024],
                        PVB[0:64, 0:1024], PBF[0:64, 1024:2048], ALU.mult)
                vgrp(("PVA", "PVB", "PB0", "PB1"), ("AO", "CTr"), vb2)

            def tb4(te):
                last = None
                for fs in range(4):
                    last = te.matmul(
                        PS[64:128, fs * 512:fs * 512 + 512],
                        IDE[0:64, 0:64],
                        CT[0:64, fs * 512:fs * 512 + 512],
                        start=True, stop=True, tile_position=(0, 64))
                return last
            tgrp(("CTr", "IDE"), ("PSa", "PSb"), tb4)
            vgrp(("PSa", "PSb"), ("AO",), lambda v, aoc=aoc:
                 v.tensor_copy(AO[64:128, aoc:aoc + 2048], PS[64:128, :]))

        def emit_l0_pair(c):
            if c + 1 < NP:
                load_l0_slices(c + 1)
            elif BUILD >= 3:
                load_l1_slices(0)
            jit_proj_fm(c, 512, 4, CTQ, "CTQ", R_BUQ, QF, "QF")
            jit_proj_fm(c, 0, 4, CTQ, "CTQ", R_BQR, CT, "CTr")
            rope_fm(c, QF[:, :], "QF")
            nb_part(0, QF, "QF")
            jit_proj_fm(c, 1024, 4, CTK, "CTK", R_BUK, KF, "KF")
            jit_proj_fm(c, 2048, 8, XNT, "XNT", R_BKR, CT, "CTr")
            rope_fm(c, KF[:, :], "KF")
            nb_part(1, KF, "KF")
            jit_v_l0(c)
            emit_attention(c, 0)

        def emit_l1_pair(c):
            if c + 1 < NP:
                load_l1_slices(c + 1)
            h = c % 2
            wreg = "WSL%d" % h
            for which, dst, dreg in ((0, QF, "QF"), (1, KF, "KF")):
                def emit_qk(which=which, dst=dst, dreg=dreg):
                    for fs in range(4):
                        half = fs % 2
                        preg = "PSa" if half == 0 else "PSb"
                        pslc = (PSa if half == 0 else PSb)[:, 0:512]

                        def tb2(te, which=which, fs=fs, pslc=pslc, h=h):
                            last = None
                            for dc in range(NCH):
                                last = te.matmul(
                                    pslc,
                                    wsl(h, which * 1024 + dc * 128, 128),
                                    XNT[:, dc * 2048 + fs * 512:
                                        dc * 2048 + fs * 512 + 512],
                                    start=(dc == 0), stop=False)
                            return te.matmul(
                                pslc,
                                BROW[0:1, B_BQK1 + which * 1024 + c * 128:
                                     B_BQK1 + which * 1024 + c * 128 + 128],
                                ONER[0:1, 0:512],
                                start=False, stop=True)
                        tgrp((wreg, "XNT", "BROW", "ONES"), (preg,), tb2)
                        evac(preg, dreg, dst[:, fs * 512:fs * 512 + 512],
                             pslc, fs % 2 == 0)
                emit_qk()
                nb_part(which, dst, dreg)
            for half in range(2):
                preg = "PSa" if half == 0 else "PSb"
                phalf = PSa if half == 0 else PSb

                def tb(te, half=half, phalf=phalf, h=h):
                    last = None
                    for tt in range(8):
                        t = half * 8 + tt
                        o = phalf[:, tt * 128:tt * 128 + 128]
                        last = te.matmul(
                            o, ONER[0:1, 0:128],
                            BROW[0:1, B_V1 + c * 128:B_V1 + c * 128 + 128],
                            start=True, stop=False)
                        for dc in range(NCH):
                            last = te.matmul(
                                o,
                                XNT[:, dc * 2048 + t * 128:
                                    dc * 2048 + t * 128 + 128],
                                wsl(h, 2048 + dc * 128, 128),
                                start=False, stop=(dc == NCH - 1))
                    return last
                tgrp(("XNT", wreg, "ONES", "BROW"), (preg,), tb)

                def vb(v, half=half, phalf=phalf):
                    dst = VP[:, :].rearrange(
                        "p (kb a u) -> p kb a u", kb=16, a=2)[
                        :, half * 8:half * 8 + 8, :, 0:64]
                    return v.tensor_copy(
                        dst, phalf[:, 0:1024].rearrange(
                            "p (t a s) -> p t a s", t=8, a=2))
                vgrp((preg,), ("VP",), vb)
            emit_attention(c, 1)

        def emit_wo(layer, halfgrp):
            wo_d = wo0_d if layer == 0 else wo1_d
            wor = wo_d.rearrange("(ch p) n -> p ch n", p=128)
            for c4 in range(4):
                ch = halfgrp * 4 + c4
                wreg = "PB%d" % c4
                dma("D2", PB[:, c4 * 1024:c4 * 1024 + 1024],
                    wor[:, ch, :], (wreg,))
            for t in range(NT):
                half = t % 2
                preg = "PSa" if half == 0 else "PSb"
                phalf = PSa if half == 0 else PSb

                def tb(te, t=t, phalf=phalf, halfgrp=halfgrp):
                    last = None
                    if halfgrp == 1:
                        for fs in range(2):
                            last = te.matmul(
                                phalf[:, fs * 512:fs * 512 + 512],
                                ONER[0:1, 0:128],
                                BROW[0:1, B_WO + fs * 512:
                                     B_WO + fs * 512 + 512],
                                start=True, stop=False)
                    for c4 in range(4):
                        for fs in range(2):
                            last = te.matmul(
                                phalf[:, fs * 512:fs * 512 + 512],
                                AO[:, c4 * 2048 + t * 128:
                                   c4 * 2048 + t * 128 + 128],
                                PB[:, c4 * 1024 + fs * 512:
                                   c4 * 1024 + fs * 512 + 512],
                                start=(c4 == 0 and halfgrp == 0),
                                stop=(c4 == 3))
                    return last
                tgrp(("AO", "PB0", "PB1", "PB2", "PB3", "ONES", "BROW"),
                     (preg,), tb)
                vgrp((preg, "X"), ("X",), lambda v, t=t, phalf=phalf:
                     v.tensor_tensor(X[:, t * D:(t + 1) * D],
                                     X[:, t * D:(t + 1) * D],
                                     phalf[:, 0:1024], ALU.add))

        def emit_l0_ffn():
            XINT = CTK[:, 0:2048].bitcast(I32)
            F2 = CTK[:, 2048:4096].bitcast(F32)
            F3 = CTK[:, 4096:6144].bitcast(F32)
            TSTG = CTK[:, 6144:8192].bitcast(I32)
            WT = CTQ[:, :]
            FBR = QF[:, 0:2048].bitcast(F32)

            dma("D2", FBR, fbr_d, ("QF",))
            def vb(v):
                v.tensor_scalar_max(STAT[0:1, 22:23], STAT[0:1, 8:9], 1e-5)
                v.drain()
                v.reciprocal(STAT[0:1, 23:24], STAT[0:1, 22:23])
                v.drain()
                v.tensor_copy(MB[0:1, 0:1], STAT[0:1, 22:23])
                return v.tensor_copy(MB[0:1, 1:2], STAT[0:1, 23:24])
            vgrp(("STATf",), ("STATf", "MB"), vb)
            def vb2(v):
                v.tensor_tensor(STAT[0:1, 24:25], STAT[0:1, 22:23],
                                MB[0:1, 0:1], ALU.subtract)
                v.tensor_tensor(STAT[0:1, 25:26], STAT[0:1, 23:24],
                                MB[0:1, 1:2], ALU.subtract)
                v.drain()
                v.tensor_copy(PB[0:1, 0:1], STAT[0:1, 24:25])
                return v.tensor_copy(PB[0:1, 1:2], STAT[0:1, 25:26])
            vgrp(("STATf", "MB"), ("STATf", "PB0"), vb2)

            def tb(te):
                te.matmul(PSa[:, 0:1], ONER[0:1, 0:128], MB[0:1, 0:1],
                          start=True, stop=False)
                te.matmul(PSa[:, 0:1], ONER[0:1, 0:128], PB[0:1, 0:1],
                          start=False, stop=True)
                te.matmul(PSa[:, 1:2], ONER[0:1, 0:128], MB[0:1, 1:2],
                          start=True, stop=False)
                return te.matmul(PSa[:, 1:2], ONER[0:1, 0:128], PB[0:1, 1:2],
                                 start=False, stop=True)
            tgrp(("MB", "PB0", "ONES"), ("PSa",), tb)
            vgrp(("PSa",), ("REP",), lambda v: (
                v.tensor_copy(REP[:, R_SREP:R_SREP + 1], PSa[:, 0:1]),
                v.tensor_copy(REP[:, R_RCP:R_RCP + 1], PSa[:, 1:2]))[-1])

            twr = tw_d.rearrange("(ob p) i -> p ob i", p=128)
            for ob in range(NCH):
                dma("D1", TSTG, twr[:, ob, :], ("CTK",))
                scgrp(("CTK", "CONSTF"), ("PB0",), lambda sc:
                      sc.activation(PB[:, 0:1024], TSTG, AF.Identity,
                                    bias=CONSTF[:, 1:2]))
                for g4 in range(2):
                    preg = "PSa" if g4 == 0 else "PSb"
                    pt = (PSa if g4 == 0 else PSb)[:, 0:256].bitcast(BF16)

                    def tb2(te, g4=g4, pt=pt):
                        last = None
                        for j in range(4):
                            ci = g4 * 4 + j
                            last = te.transpose(
                                pt[:, j * 128:j * 128 + 128],
                                PB[:, ci * 128:ci * 128 + 128], IDE[:, :])
                        return last
                    tgrp(("PB0", "IDE"), (preg,), tb2)

                    def vb3(v, ob=ob, g4=g4, pt=pt):
                        dst = WT.rearrange("p (ci u) -> p ci u", ci=NCH)[
                            :, g4 * 4:g4 * 4 + 4, ob * 128:ob * 128 + 128]
                        return v.tensor_copy(
                            dst, pt.rearrange("p (ci u) -> p ci u", ci=4))
                    vgrp((preg,), ("CTQ",), vb3)

            dbg("ffnrep", REP[:, :], (128, 64), ("REP",))
            dbg("ffnwt", CTQ[:, :], (128, 8192), ("CTQ",))
            emit_ln_stats_all()
            for t in range(NT):
                ln_apply(t, F1[:, :], "F1")

                def vb4(v, t=t):
                    v.tensor_scalar(F2[:, 0:1024], F1[:, :],
                                    REP[:, R_RCP:R_RCP + 1], 127.0,
                                    ALU.mult, ALU.min)
                    v.drain()
                    v.tensor_scalar_max(F3[:, 0:1024], F2[:, 0:1024], -127.0)
                    v.drain()
                    v.tensor_copy(XINT[:, 0:1024], F3[:, 0:1024])
                    v.drain()
                    lreg = RB[:, (t % 2) * D:(t % 2) * D + D]
                    return v.tensor_copy(lreg, XINT[:, 0:1024])
                vgrp(("CTK", "F1", "REP"), ("CTK", "RB%d" % (t % 2)), vb4)
                for half in range(2):
                    preg = "PSa" if half == 0 else "PSb"
                    pt = (PSa if half == 0 else PSb)[:, 0:256].bitcast(BF16)

                    def tb3(te, t=t, half=half, pt=pt):
                        last = None
                        for cc in range(4):
                            c = half * 4 + cc
                            last = te.transpose(
                                pt[:, cc * 128:cc * 128 + 128],
                                RB[:, (t % 2) * D + c * 128:
                                   (t % 2) * D + c * 128 + 128], IDE[:, :])
                        return last
                    tgrp(("RB%d" % (t % 2), "IDE"), (preg,), tb3)

                    def vb5(v, t=t, half=half, pt=pt):
                        dst = XNT[:, :].rearrange("p (c s) -> p c s", c=NCH)[
                            :, half * 4:half * 4 + 4, t * 128:t * 128 + 128]
                        return v.tensor_copy(
                            dst, pt.rearrange("p (c s) -> p c s", c=4))
                    vgrp((preg,), ("XNT",), vb5)
                half = t % 2
                preg = "PSa" if half == 0 else "PSb"
                phalf = PSa if half == 0 else PSb

                def tb4(te, t=t, phalf=phalf):
                    last = None
                    for fs in range(2):
                        for ci in range(NCH):
                            last = te.matmul(
                                phalf[:, fs * 512:fs * 512 + 512],
                                XNT[:, ci * 2048 + t * 128:
                                    ci * 2048 + t * 128 + 128],
                                WT[:, ci * 1024 + fs * 512:
                                   ci * 1024 + fs * 512 + 512],
                                start=(ci == 0), stop=(ci == NCH - 1))
                    return last
                tgrp(("XNT", "CTQ"), (preg,), tb4)

                vgrp((preg, "REP", "QF", "F1"), ("F1",),
                     lambda v, phalf=phalf: v.scalar_tensor_tensor(
                         F1[:, :], phalf[:, 0:1024],
                         REP[:, R_SREP:R_SREP + 1], FBR[:, 0:1024],
                         ALU.mult, ALU.add))
                ggrp(("F1", "X"), ("X",), lambda g, t=t: g.tensor_tensor(
                    X[:, t * D:(t + 1) * D], X[:, t * D:(t + 1) * D],
                    F1[:, :], ALU.add))
                if t == 0:
                    dbg("ffnxq", XNT[:, :], (128, NCH * S), ("XNT",))
                    dbg("ffnq0", RB[:, :], (128, 2048), ("RB0", "RB1"))

        def emit_l1_ffn():
            ST = CTQ[0:64, 0:2048]
            EMS = CTQ[0:64, 2048:3072]
            W2E = CTQ[:, 3072:3584]
            WINS = CTQ[:, 3584:4096]
            WOUTS = CTQ[0:64, 4096:5120]
            GT = CTQ[0:64, 5120:7168]
            GTOK = CTQ[:, 7168:8192]
            E2S = CTK[:, 0:512]

            dma("D2", WINS.rearrange("p (dc m) -> p dc m", dc=NCH),
                win_d.rearrange("(dc p) m -> p dc m", p=128), ("CTQ",))
            dma("D2", EMS, emat_d, ("CTQ",))
            dma("D2", E2S.rearrange("p (ci m) -> p ci m", ci=NCH),
                e2_d.rearrange("(ci p) m -> p ci m", p=128), ("CTK",))
            dma("D2", STAT[0:64, 28:29], binc_d, ("STATb",))
            dma("D2", REP[:, R_W1C:R_W1C + 8], w1c_d, ("REP",))
            dma("D2", REP[:, R_B1C:R_B1C + 8], b1c_d, ("REP",))
            dma("D2", REP[:, R_W2C:R_W2C + 8], w2c_d, ("REP",))
            dma("D2", BROW[:, B_FO:B_FO + D + 64], browf_d, ("BROW",))
            dma("D2", WOUTS, wout1_d, ("CTQ",))

            def vb(v):
                last = None
                for ci in range(NCH):
                    last = v.tensor_scalar(
                        W2E[:, ci * 64:ci * 64 + 64],
                        E2S[:, ci * 64:ci * 64 + 64],
                        REP[:, R_W2C + ci:R_W2C + ci + 1], None, ALU.mult)
                return last
            vgrp(("CTK", "REP"), ("CTQ",), vb)

            emit_ln_to_xnt()

            for fs in range(4):
                half = fs % 2
                preg = "PSa" if half == 0 else "PSb"
                pslc = (PSa if half == 0 else PSb)[0:64, 0:512]

                def tb(te, fs=fs, pslc=pslc):
                    last = None
                    for dc in range(NCH):
                        last = te.matmul(
                            pslc, WINS[:, dc * 64:dc * 64 + 64],
                            XNT[:, dc * 2048 + fs * 512:
                                dc * 2048 + fs * 512 + 512],
                            start=(dc == 0), stop=(dc == NCH - 1))
                    return last
                tgrp(("CTQ", "XNT"), (preg,), tb)
                scgrp((preg, "STATb"), ("CTQ",), lambda sc, fs=fs, pslc=pslc:
                      sc.activation(ST[:, fs * 512:fs * 512 + 512], pslc,
                                    AF.Identity, bias=STAT[0:64, 28:29]))

            for ci in range(NCH):
                def tb2(te, ci=ci):
                    last = None
                    for fs in range(4):
                        last = te.matmul(
                            PS[:, fs * 512:fs * 512 + 512],
                            EMS[:, ci * 128:ci * 128 + 128],
                            ST[:, fs * 512:fs * 512 + 512],
                            start=True, stop=True)
                    return last
                tgrp(("CTQ",), ("PSa", "PSb"), tb2)
                scgrp(("PSa", "PSb", "REP"), ("XNT",), lambda sc, ci=ci:
                      sc.activation(XNT[:, ci * 2048:ci * 2048 + 2048],
                                    PS[:, :], AF.Relu,
                                    scale=REP[:, R_W1C + ci:R_W1C + ci + 1],
                                    bias=REP[:, R_B1C + ci:R_B1C + ci + 1]))

            for half in range(2):
                preg = "PSa" if half == 0 else "PSb"
                phalf = PSa if half == 0 else PSb

                def tb3(te, half=half, phalf=phalf):
                    last = None
                    for tt in range(8):
                        t = half * 8 + tt
                        o = phalf[:, tt * 64:tt * 64 + 64]
                        last = te.matmul(o, ONER[0:1, 0:128],
                                         BROW[0:1, B_B2:B_B2 + 64],
                                         start=True, stop=False)
                        for ci in range(NCH):
                            last = te.matmul(
                                o,
                                XNT[:, ci * 2048 + t * 128:
                                    ci * 2048 + t * 128 + 128],
                                W2E[:, ci * 64:ci * 64 + 64],
                                start=False, stop=(ci == NCH - 1))
                    return last
                tgrp(("XNT", "CTQ", "ONES", "BROW"), (preg,), tb3)
                vgrp((preg,), ("CTQ",), lambda v, half=half, phalf=phalf:
                     v.tensor_copy(GTOK[:, half * 512:half * 512 + 512],
                                   phalf[:, 0:512]))
            for g4 in range(4):
                half = g4 % 2
                preg = "PSa" if half == 0 else "PSb"
                pt = (PSa if half == 0 else PSb)[0:64, 0:256].bitcast(BF16)

                def tb4(te, g4=g4, pt=pt):
                    last = None
                    for j in range(4):
                        t = g4 * 4 + j
                        last = te.transpose(pt[:, j * 128:j * 128 + 128],
                                            GTOK[:, t * 64:t * 64 + 64],
                                            IDE[:, :])
                    return last
                tgrp(("CTQ", "IDE"), (preg,), tb4)
                vgrp((preg,), ("CTQ",), lambda v, g4=g4, pt=pt:
                     v.tensor_copy(GT[:, g4 * 512:g4 * 512 + 512], pt))

            for t in range(NT):
                half = t % 2
                preg = "PSa" if half == 0 else "PSb"
                phalf = PSa if half == 0 else PSb

                def tb5(te, t=t, phalf=phalf):
                    last = None
                    for fs in range(2):
                        o = phalf[:, fs * 512:fs * 512 + 512]
                        last = te.matmul(o, ONER[0:1, 0:128],
                                         BROW[0:1, B_FO + fs * 512:
                                              B_FO + fs * 512 + 512],
                                         start=True, stop=False)
                        last = te.matmul(
                            o, GT[:, t * 128:t * 128 + 128],
                            WOUTS[:, fs * 512:fs * 512 + 512],
                            start=False, stop=True)
                    return last
                tgrp(("CTQ", "ONES", "BROW"), (preg,), tb5)
                vgrp((preg, "X"), ("X",), lambda v, t=t, phalf=phalf:
                     v.tensor_tensor(X[:, t * D:(t + 1) * D],
                                     X[:, t * D:(t + 1) * D],
                                     phalf[:, 0:1024], ALU.add))

        load_l0_slices(0)
        emit_ln_to_xnt()
        dbg("xnt", XNT[:, :], (128, NCH * S), ("XNT",))
        emit_ct()
        dbg("ctq", CTQ[:, :], (128, 8192), ("CTQ",))
        dbg("ctk", CTK[:, :], (128, 8192), ("CTK",))
        for c in range(NP):
            emit_l0_pair(c)
            if c == 0 and dbg_on:
                dbg("qf0", QF[:, :], (128, 2048), ("QF",))
                dbg("kf0", KF[:, :], (128, 2048), ("KF",))
                dbg("vp0", VP[:, :], (128, 2080), ("VP",))
                dbg("rep0", REP[:, :], (128, 64), ("REP",))
                dbg("f1r", F1[:, :], (128, 1024), ("F1",))
                dbg("ao0", AO[:, 0:2048], (128, 2048), ("AO",))
            if c == 3:
                if dbg_on:
                    dbg("aoh0", AO[:, :], (128, 8192), ("AO",))
                emit_wo(0, 0)
            if c == 7 and dbg_on:
                dbg("aoh1", AO[:, :], (128, 8192), ("AO",))
        emit_wo(0, 1)
        dbg("xfin", X[:, :], (128, NT * D), ("X",))

        if BUILD >= 2:
            emit_l0_ffn()

        if BUILD >= 3:
            dma("D2", BROW[:, B_V1:B_V1 + D], bv1_d, ("BROW",))
            dma("D2", BROW[:, B_WO:B_WO + D], bwo1_d, ("BROW",))
            dma("D2", BROW[:, B_BQK1:B_BQK1 + 2 * D], bqk_d, ("BROW",))
            emit_ln_to_xnt()
            for c in range(NP):
                emit_l1_pair(c)
                if c == 3:
                    emit_wo(1, 0)
            emit_wo(1, 1)

        if BUILD >= 4:
            emit_l1_ffn()

        pend = []
        res.rd(pend, "D1", "X")
        final_tok = sem.reserve("D1", 16)

        def final_store(d1, pend=pend, final_tok=final_tok):
            for s, t in pend:
                d1.wait_ge(sem.h[s], t)
            i = d1.dma_start(out_d.rearrange("(t p) d -> p t d", p=128),
                             X[:, :].rearrange("p (t d) -> p t d", t=NT))
            i.then_inc(sem.h["D1"], 16)
            d1.wait_ge(sem.h["D1"], final_tok)
            return i
        STREAMS["D1"].append(([], final_store, [], False))

        flush_streams()

    return nc


_NC_CACHE = {}
DBG_OUT = {}


def _theta():
    ids = np.arange(512, dtype=np.float64)
    return (10000.0 ** (-2.0 * ids) / 512.0)


def _perm():
    p = []
    for h in range(16):
        p.extend(range(h * 64, h * 64 + 64, 2))
        p.extend(range(h * 64 + 1, h * 64 + 64, 2))
    return np.array(p)


def kernel(L0_attn_x, L0_attn_Wd_q_w, L0_attn_Wd_q_b, L0_attn_Wu_q_w,
           L0_attn_Wu_q_b, L0_attn_Wq_r_w, L0_attn_Wq_r_b, L0_attn_Wk_r_w,
           L0_attn_Wk_r_b, L0_attn_Wd_kv_w, L0_attn_Wd_kv_b, L0_attn_Wu_k_w,
           L0_attn_Wu_k_b, L0_attn_Wu_v_w, L0_attn_Wu_v_b, L0_attn_Wo_w,
           L0_attn_Wo_b, L0_ffn_ternary_weight, L0_ffn_bias, L0_ffn_act_scale,
           L1_attn_W_qkv, L1_attn_b_qkv, L1_attn_W_out, L1_attn_b_out,
           L1_ffn_Win, L1_ffn_bin_, L1_ffn_W1, L1_ffn_b1, L1_ffn_W2,
           L1_ffn_b2, L1_ffn_Wout, L1_ffn_bout):
    f32 = np.float32
    perm = _perm()

    def c(a):
        return np.ascontiguousarray(np.asarray(a), dtype=f32)

    theta = _theta()
    pos = np.arange(S, dtype=np.float64)
    rts = np.empty((NP, 128, S), dtype=BF)
    r_idx = np.arange(128)
    hd, kk, ii = r_idx // 64, (r_idx % 64) // 32, r_idx % 32
    sign = np.where(kk == 0, -1.0, 1.0)
    for cc in range(NP):
        th = theta[64 * cc + hd * 32 + ii]
        rts[cc] = (sign[:, None] *
                   np.sin(pos[None, :] * th[:, None])).astype(BF)
    rc0 = np.zeros((33, S), dtype=f32)
    c0m1 = np.cos(pos * theta[0]) - 1.0
    rc0[0, :] = c0m1
    rc0[32, :] = c0m1
    swp = np.zeros((128, 128), dtype=f32)
    jj = np.where((r_idx % 64) < 32, r_idx + 32, r_idx - 32)
    swp[r_idx, jj] = 1.0

    bqr = c(L0_attn_Wq_r_b)[perm]
    buq = c(L0_attn_Wu_q_b)[perm]
    bkr = c(L0_attn_Wk_r_b)[perm]
    buk = c(L0_attn_Wu_k_b)[perm]
    brow0 = np.concatenate([bqr, buq, bkr, buk, c(L0_attn_Wu_v_b)])[None, :]

    b_qkv = c(L1_attn_b_qkv)

    common = dict(
        wdq=c(L0_attn_Wd_q_w),
        wdkv=c(L0_attn_Wd_kv_w),
        wuq=np.ascontiguousarray(c(L0_attn_Wu_q_w)[:, perm]),
        wqr=np.ascontiguousarray(c(L0_attn_Wq_r_w)[:, perm]),
        wuk=np.ascontiguousarray(c(L0_attn_Wu_k_w)[:, perm]),
        wkr=np.ascontiguousarray(c(L0_attn_Wk_r_w)[:, perm]),
        wuv=c(L0_attn_Wu_v_w),
        wo0=c(L0_attn_Wo_w),
        bdq_t=np.ascontiguousarray(c(L0_attn_Wd_q_b).reshape(4, 128).T),
        bdkv_t=np.ascontiguousarray(c(L0_attn_Wd_kv_b).reshape(4, 128).T),
        brow0=np.ascontiguousarray(brow0),
        bwo0=c(L0_attn_Wo_b)[None, :],
        rts=rts,
        rc0=rc0,
        swp=swp,
        bqr_c=np.ascontiguousarray(bqr.reshape(8, 128).T),
        buq_c=np.ascontiguousarray(buq.reshape(8, 128).T),
        bkr_c=np.ascontiguousarray(bkr.reshape(8, 128).T),
        buk_c=np.ascontiguousarray(buk.reshape(8, 128).T),
        tw=np.ascontiguousarray(np.asarray(L0_ffn_ternary_weight,
                                           dtype=np.int32)),
        fb_rep=np.ascontiguousarray(np.broadcast_to(c(L0_ffn_bias)[None, :],
                                                    (128, D))),
        ascale=c(L0_ffn_act_scale).reshape(1, 1),
        wqkv=c(L1_attn_W_qkv),
        bqk_row=np.ascontiguousarray(b_qkv[0:2048][None, :]),
        bv1=np.ascontiguousarray(b_qkv[2048:3072][None, :]),
        wo1=c(L1_attn_W_out),
        bwo1=c(L1_attn_b_out)[None, :],
        win=c(L1_ffn_Win),
        bin_c=c(L1_ffn_bin_).reshape(64, 1),
        w1c=np.ascontiguousarray(c(L1_ffn_W1).reshape(1024).reshape(8, 128).T),
        b1c=np.ascontiguousarray(c(L1_ffn_b1).reshape(1024).reshape(8, 128).T),
        emat=np.ascontiguousarray(
            (np.arange(D)[None, :] // 16 == np.arange(64)[:, None])
            .astype(f32)),
        e2=np.ascontiguousarray(
            (np.arange(D)[:, None] // 16 == np.arange(64)[None, :])
            .astype(f32)),
        w2c=np.ascontiguousarray(c(L1_ffn_W2).reshape(1024).reshape(8, 128).T),
        browf=np.concatenate([c(L1_ffn_bout), c(L1_ffn_b2)])[None, :],
        wout1=c(L1_ffn_Wout),
        ident=np.eye(128, dtype=f32),
        cmask=np.ascontiguousarray(
            (np.arange(128)[None, :] >= np.arange(128)[:, None])
            .astype(f32)),
    )

    if "nc" not in _NC_CACHE:
        _NC_CACHE["nc"] = build_program()
    nc = _NC_CACHE["nc"]

    x = np.asarray(L0_attn_x, dtype=f32)
    in_maps = []
    for i in range(8):
        m = dict(common)
        m["x"] = np.ascontiguousarray(x[i])
        in_maps.append(m)
    r = run_bass_kernel_spmd(nc, in_maps, core_ids=list(range(8)))
    if os.environ.get("KDBG"):
        DBG_OUT.clear()
        DBG_OUT.update({k: np.asarray(v) for k, v in r.results[0].items()
                        if k.startswith("dbg_")})
    return np.stack([np.asarray(rr["out"]) for rr in r.results], axis=0)


# version 22
# speedup vs baseline: 1.0528x; 1.0528x over previous
import numpy as np
import ml_dtypes
from contextlib import ExitStack

import concourse.bass as bass
import concourse.mybir as mybir
from concourse.bass_utils import run_bass_kernel_spmd

F32 = mybir.dt.float32
BF16 = mybir.dt.bfloat16
I32 = mybir.dt.int32
AF = mybir.ActivationFunctionType
ALU = mybir.AluOpType
AX = mybir.AxisListType

S = 2048
D = 1024
NT = 16
NCH = 8
NP = 8
NKB = 16
DQ = 512
L0_SCALE = float(1.0 / np.sqrt(np.float64(16 + 512 + 512)))
L1_SCALE = 0.125
BF = ml_dtypes.bfloat16

import os
BUILD = int(os.environ.get("KBUILD", "4"))


class Sems:
    def __init__(self, ctx, nc, names):
        self.h = {}
        self.c = {}
        for n in names:
            self.h[n] = ctx.enter_context(nc.semaphore(n))
            self.c[n] = 0

    def reserve(self, name, n=1):
        self.c[name] += n
        return self.c[name]


class Res:

    def __init__(self, sems):
        self.sems = sems
        self.w = {}
        self.r = {}
        self.waited = {}

    def _wait(self, pend, ename, sname, tok):
        if sname == ename:
            return
        wd = self.waited.setdefault(ename, {})
        if wd.get(sname, 0) >= tok:
            return
        pend.append((sname, tok))
        wd[sname] = tok

    def rd(self, pend, ename, *regions):
        for reg in regions:
            if reg in self.w:
                s, t = self.w[reg]
                self._wait(pend, ename, s, t)

    def wr(self, pend, ename, *regions):
        for reg in regions:
            if reg in self.w:
                s, t = self.w[reg]
                self._wait(pend, ename, s, t)
            for s, t in self.r.get(reg, {}).items():
                self._wait(pend, ename, s, t)

    def did_rd(self, ename, tok, *regions):
        for reg in regions:
            self.r.setdefault(reg, {})[ename] = tok

    def did_wr(self, ename, tok, *regions):
        for reg in regions:
            self.w[reg] = (ename, tok)
            self.r[reg] = {}


def build_program():
    nc = bass.Bass("TRN2", target_bir_lowering=False, debug=False)

    def din(name, shape, dt=F32):
        return nc.dram_tensor(name, list(shape), dt, kind="ExternalInput").ap()

    x_in = din("x", (S, D))
    out_d = nc.dram_tensor("out", [S, D], F32, kind="ExternalOutput").ap()

    wdq_d = din("wdq", (D, DQ))
    wdkv_d = din("wdkv", (D, DQ))
    wuq_d = din("wuq", (DQ, D))
    wqr_d = din("wqr", (DQ, D))
    wuk_d = din("wuk", (DQ, D))
    wkr_d = din("wkr", (D, D))
    wuv_d = din("wuv", (DQ, D))
    wo0_d = din("wo0", (D, D))
    bdq_d = din("bdq_t", (128, 4))
    bdkv_d = din("bdkv_t", (128, 4))
    brow0_d = din("brow0", (1, 5 * D))
    bwo0_d = din("bwo0", (1, D))
    rts_d = din("rts", (NP, 128, 2048), BF16)
    rc0_d = din("rc0", (33, 2048))
    swp_d = din("swp", (128, 128))
    bqrc_d = din("bqr_c", (128, 8))
    buqc_d = din("buq_c", (128, 8))
    bkrc_d = din("bkr_c", (128, 8))
    bukc_d = din("buk_c", (128, 8))

    tw_d = din("tw", (D, D), I32)
    fbr_d = din("fb_rep", (128, D))
    asc_d = din("ascale", (1, 1))

    wqkv_d = din("wqkv", (D, 3 * D))
    bqk_d = din("bqk_row", (1, 2 * D))
    bv1_d = din("bv1", (1, D))
    wo1_d = din("wo1", (D, D))
    bwo1_d = din("bwo1", (1, D))

    win_d = din("win", (D, 64))
    binc_d = din("bin_c", (64, 1))
    w1c_d = din("w1c", (128, 8))
    b1c_d = din("b1c", (128, 8))
    emat_d = din("emat", (64, D))
    e2_d = din("e2", (D, 64))
    w2c_d = din("w2c", (128, 8))
    browf_d = din("browf", (1, D + 64))
    wout1_d = din("wout1", (64, D))

    ident_d = din("ident", (128, 128))
    cmask_d = din("cmask", (128, 128))

    ctx = ExitStack()
    with ctx:
        E = ctx.enter_context
        X = E(nc.sbuf_tensor("X", [128, NT * D], F32))
        XNT = E(nc.sbuf_tensor("XNT", [128, NCH * S], BF16))
        CTQ = E(nc.sbuf_tensor("CTQ", [128, 8192], BF16))
        CTK = E(nc.sbuf_tensor("CTK", [128, 8192], BF16))
        AO = E(nc.sbuf_tensor("AO", [128, 8192], BF16))
        RB = E(nc.sbuf_tensor("RB", [128, 2048], BF16))
        CT = E(nc.sbuf_tensor("CT", [128, 2048], BF16))
        QF = E(nc.sbuf_tensor("QF", [128, 2048], BF16))
        KF = E(nc.sbuf_tensor("KF", [128, 2048], BF16))
        VP = E(nc.sbuf_tensor("VP", [128, 2080], BF16))
        PB = E(nc.sbuf_tensor("PB", [128, 4096], BF16))
        WSL = E(nc.sbuf_tensor("WSL", [128, 6144], BF16))
        RTS = E(nc.sbuf_tensor("RTS", [128, 2048], BF16))
        SWP = E(nc.sbuf_tensor("SWP", [128, 128], BF16))
        BROW = E(nc.sbuf_tensor("BROW", [1, 6144], BF16))
        F1 = E(nc.sbuf_tensor("F1", [128, 1024], F32))
        STAT = E(nc.sbuf_tensor("STAT", [128, 32], F32))
        ST2 = E(nc.sbuf_tensor("ST2", [128, 272], F32))
        REP = E(nc.sbuf_tensor("REP", [128, 96], F32))
        MB = E(nc.sbuf_tensor("MB", [128, 2], BF16))
        IDE = E(nc.sbuf_tensor("IDE", [128, 128], BF16))
        CMK = E(nc.sbuf_tensor("CMK", [128, 128], BF16))
        ONER = E(nc.sbuf_tensor("ONER", [1, 512], BF16))
        ONEC = E(nc.sbuf_tensor("ONEC", [128, 1], BF16))
        ONERF = E(nc.sbuf_tensor("ONERF", [65, 64], F32))
        CONSTF = E(nc.sbuf_tensor("CONSTF", [128, 2], F32))

        PS = E(nc.psum_tensor("PS", [128, 2048], F32))
        PVA = E(nc.psum_tensor("PVA", [65, 1024], F32))
        PVB = E(nc.psum_tensor("PVB", [65, 1024], F32))

        sem = Sems(ctx, nc, ["T", "V", "Sc", "D1", "D2", "G"])
        res = Res(sem)
        blk = E(nc.Block())

        def T(f):
            blk.tensor(f)

        def V(f):
            blk.vector(f)

        def SC(f):
            blk.scalar(f)

        def D1(f):
            blk.sync(f)

        def D2(f):
            blk.gpsimd(f)

        B_QR, B_UQ, B_KR, B_UK, B_UV = 0, 1024, 2048, 3072, 4096
        B_WO = 5120
        B_V1, B_FO, B_B2 = 0, 1024, 2048
        B_BQK1 = 2048
        R_BA, R_SREP, R_RCP = 0, 1, 2
        R_BDQ, R_BDKV = 4, 8
        R_W1C, R_B1C, R_W2C = 32, 40, 48
        R_BQR, R_BUQ, R_BKR, R_BUK = 64, 72, 80, 88

        PSa = PS[:, 0:1024]
        PSb = PS[:, 1024:2048]

        def wsl(h, off, n):
            return WSL[:, h * 3072 + off:h * 3072 + off + n]

        def evac(preg, dreg, dst, src, use_sc):
            if use_sc:
                scgrp((preg,), (dreg,),
                      lambda sc, dst=dst, src=src: sc.copy(dst, src))
            else:
                vgrp((preg,), (dreg,),
                     lambda v, dst=dst, src=src: v.tensor_copy(dst, src))

        STREAMS = {"T": [], "V": [], "Sc": [], "D1": [], "D2": []}
        ORDER = []

        _EXP = {"PSa": ("PSa0", "PSa1"), "PSb": ("PSb0", "PSb1")}

        def _expand(regions):
            out = []
            for r in regions:
                out.extend(_EXP.get(r, (r,)))
            return out

        def _grp(stream, ename, sname, reads, writes, body, drain=False,
                 inc_n=1):
            reads = _expand(reads)
            writes = _expand(writes)
            pend = []
            res.rd(pend, ename, *reads)
            res.wr(pend, ename, *writes)
            tok = sem.reserve(sname, inc_n)
            STREAMS[stream].append((pend, body, [(sname, inc_n)], drain))
            ORDER.append((stream, STREAMS[stream][-1]))
            res.did_rd(ename, tok, *reads)
            res.did_wr(ename, tok, *writes)

        def tgrp(reads, writes, body):
            _grp("T", "T", "T", reads, writes, body)

        def vgrp(reads, writes, body, drain=False):
            _grp("V", "V", "V", reads, writes, body, drain)

        def scgrp(reads, writes, body, drain=False):
            _grp("Sc", "Sc", "Sc", reads, writes, body, drain)

        def ggrp(reads, writes, body):
            _grp("D2", "G", "G", reads, writes, body)

        def dma(q, dst, src, writes, reads=()):
            _grp(q, q, q, reads, writes,
                 lambda d, dst=dst, src=src: d.dma_start(dst, src),
                 inc_n=16)

        dbg_on = bool(os.environ.get("KDBG"))
        dbg_names = []

        def dbg(name, ap, shape, reads):
            if not dbg_on:
                return
            dt = nc.dram_tensor("dbg_" + name, list(shape),
                                ap.tensor.dtype,
                                kind="ExternalOutput").ap()
            dbg_names.append(name)
            pend = []
            res.rd(pend, "D1", *reads)
            tok = sem.reserve("D1", 16)
            STREAMS["D1"].append((pend,
                                  lambda d, dt=dt, ap=ap: d.dma_start(dt, ap),
                                  [("D1", 16)], False))
            res.did_rd("D1", tok, *reads)

        def flush_streams():
            def run(eng, lst):
                for waits, body, incs, drain in lst:
                    for s, t in waits:
                        eng.wait_ge(sem.h[s], t)
                    last = body(eng)
                    if drain:
                        last = eng.drain()
                    for s, n in incs:
                        last.then_inc(sem.h[s], n)
            sect = {"T": T, "V": V, "Sc": SC, "D1": D1, "D2": D2}
            if os.environ.get("KSECT") == "many":
                for stream, entry in ORDER:
                    sect[stream](lambda e, entry=entry: run(e, [entry]))
                tail = STREAMS["D1"][len([1 for s, _ in ORDER
                                          if s == "D1"]):]
                if tail:
                    D1(lambda d: run(d, tail))
                return
            T(lambda te: run(te, STREAMS["T"]))
            V(lambda v: run(v, STREAMS["V"]))
            SC(lambda sc: run(sc, STREAMS["Sc"]))
            D1(lambda d: run(d, STREAMS["D1"]))
            D2(lambda d: run(d, STREAMS["D2"]))

        dma("D1", X[:, :].rearrange("p (t d) -> p t d", t=NT),
            x_in.rearrange("(t p) d -> p t d", p=128), ("X",))
        dma("D2", IDE[:, :], ident_d, ("IDE",))
        dma("D2", CMK[:, :], cmask_d, ("CMK",))
        RC0 = F1[0:33, 0:1024].bitcast(BF16)
        dma("D2", RC0, rc0_d, ("F1",))
        dma("D2", SWP[:, :], swp_d, ("SWP",))
        dma("D2", REP[:, R_BQR:R_BQR + 8], bqrc_d, ("REP",))
        dma("D2", REP[:, R_BUQ:R_BUQ + 8], buqc_d, ("REP",))
        dma("D2", REP[:, R_BKR:R_BKR + 8], bkrc_d, ("REP",))
        dma("D2", REP[:, R_BUK:R_BUK + 8], bukc_d, ("REP",))
        dma("D2", BROW[:, 0:5 * D], brow0_d, ("BROW",))
        dma("D2", BROW[:, B_WO:B_WO + D], bwo0_d, ("BROW",))
        dma("D2", STAT[0:1, 8:9], asc_d, ("STATf",))
        dma("D2", REP[:, R_BDQ:R_BDQ + 4], bdq_d, ("REP",))
        dma("D2", REP[:, R_BDKV:R_BDKV + 4], bdkv_d, ("REP",))

        vgrp((), ("ONES", "VP", "CONSTF"), lambda v: (
            v.memset(ONER[:, :], 1.0),
            v.memset(ONEC[:, :], 1.0),
            v.memset(ONERF[:, :], 1.0),
            v.memset(CONSTF[:, 0:1], 1e-5),
            v.memset(CONSTF[:, 1:2], -1.0),
            v.memset(VP[:, :], 1.0))[-1])

        LN_MV, LN_STD, LN_RSTD, LN_NMR = 192, 224, 240, 256

        def emit_ln_stats_all():
            def vb(v):
                last = None
                for t in range(NT):
                    v.bn_stats(ST2[:, t * 12:t * 12 + 6],
                               X[:, t * D:t * D + 512])
                    last = v.bn_stats(ST2[:, t * 12 + 6:t * 12 + 12],
                                      X[:, t * D + 512:t * D + 1024])
                return last
            vgrp(("X",), ("BNS",), vb, drain=True)

            def vb2(v):
                last = None
                for t in range(NT):
                    last = v.bn_aggr(
                        ST2[:, LN_MV + 2 * t:LN_MV + 2 * t + 2],
                        ST2[:, t * 12:t * 12 + 12])
                return last
            vgrp(("BNS",), ("LNV",), vb2, drain=True)
            varv = ST2[:, LN_MV:LN_MV + 32].rearrange(
                "p (t k) -> p t k", k=2)[:, :, 1:2]
            stdv = ST2[:, LN_STD:LN_STD + 16].rearrange(
                "p (t k) -> p t k", k=1)
            scgrp(("LNV", "CONSTF"), ("LNV",), lambda sc: (
                sc.activation(stdv, varv, AF.Ln, bias=CONSTF[:, 0:1]),
                sc.drain(),
                sc.activation(ST2[:, LN_RSTD:LN_RSTD + 16],
                              ST2[:, LN_STD:LN_STD + 16], AF.Exp,
                              scale=-0.5))[-1])
            meanv = ST2[:, LN_MV:LN_MV + 32].rearrange(
                "p (t k) -> p t k", k=2)[:, :, 0:1]
            nmrv = ST2[:, LN_NMR:LN_NMR + 16].rearrange(
                "p (t k) -> p t k", k=1)
            vgrp(("LNV",), ("LNV",), lambda v: v.scalar_tensor_tensor(
                nmrv, meanv, -1.0,
                ST2[:, LN_RSTD:LN_RSTD + 16].rearrange(
                    "p (t k) -> p t k", k=1),
                ALU.mult, ALU.mult), drain=True)

        def ln_apply(t, dst, dreg):
            scgrp(("X", "LNV"), (dreg,), lambda sc: sc.activation(
                dst, X[:, t * D:(t + 1) * D], AF.Identity,
                bias=ST2[:, LN_NMR + t:LN_NMR + t + 1],
                scale=ST2[:, LN_RSTD + t:LN_RSTD + t + 1]))

        def emit_ln_to_xnt():
            emit_ln_stats_all()
            for t in range(NT):
                lreg = "RB%d" % (t % 2)
                lnb = RB[:, (t % 2) * D:(t % 2) * D + D]
                ln_apply(t, lnb, lreg)
                for half in range(2):
                    preg = "PSa" if half == 0 else "PSb"
                    pt = (PSa if half == 0 else PSb)[:, 0:256].bitcast(BF16)

                    def tb(te, lnb=lnb, half=half, pt=pt):
                        last = None
                        for cc in range(4):
                            c = half * 4 + cc
                            last = te.transpose(
                                pt[:, cc * 128:cc * 128 + 128],
                                lnb[:, c * 128:c * 128 + 128], IDE[:, :])
                        return last
                    tgrp((lreg, "IDE"), (preg,), tb)

                    def vb(v, t=t, half=half, pt=pt):
                        dst = XNT[:, :].rearrange(
                            "p (c s) -> p c s", c=NCH)[
                            :, half * 4:half * 4 + 4,
                            t * 128:t * 128 + 128]
                        return v.tensor_copy(
                            dst, pt.rearrange("p (c s) -> p c s", c=4))
                    vgrp((preg,), ("XNT",), vb)

        def emit_ct():
            dma("D2", AO[:, 0:4096].rearrange("p (c n) -> p c n", c=NCH),
                wdq_d.rearrange("(c p) n -> p c n", p=128), ("AO",))
            dma("D2", AO[:, 4096:8192].rearrange("p (c n) -> p c n", c=NCH),
                wdkv_d.rearrange("(c p) n -> p c n", p=128), ("AO",))
            for which in range(2):
                wof = 4096 * which
                dst = CTQ if which == 0 else CTK
                dreg = "CTQ" if which == 0 else "CTK"
                bcol = R_BDQ if which == 0 else R_BDKV
                for cs in range(4):
                    for fs in range(4):
                        half = (cs * 4 + fs) % 2
                        preg = "PSa" if half == 0 else "PSb"
                        pslc = (PSa if half == 0 else PSb)[:, 0:512]

                        def tb(te, wof=wof, cs=cs, fs=fs, pslc=pslc):
                            last = None
                            for dc in range(NCH):
                                last = te.matmul(
                                    pslc,
                                    AO[:, wof + dc * 512 + cs * 128:
                                       wof + dc * 512 + cs * 128 + 128],
                                    XNT[:, dc * 2048 + fs * 512:
                                        dc * 2048 + fs * 512 + 512],
                                    start=(dc == 0), stop=(dc == NCH - 1))
                            return last
                        tgrp(("AO", "XNT"), (preg,), tb)
                        scgrp((preg, "REP"), (dreg,),
                              lambda sc, dst=dst, cs=cs, fs=fs, pslc=pslc,
                              bcol=bcol: sc.activation(
                                  dst[:, cs * 2048 + fs * 512:
                                      cs * 2048 + fs * 512 + 512],
                                  pslc, AF.Identity,
                                  bias=REP[:, bcol + cs:bcol + cs + 1]))

        def load_l0_slices(c):
            h = c % 2
            wreg = "WSL%d" % h
            sl = (c * 128, (c + 1) * 128)
            dma("D2", wsl(h, 0, 512).rearrange("p (cc j) -> p cc j", cc=4),
                wqr_d.rearrange("(cc p) d -> p cc d", p=128)[:, :, sl[0]:sl[1]],
                (wreg,))
            dma("D2", wsl(h, 512, 512).rearrange("p (cc j) -> p cc j", cc=4),
                wuq_d.rearrange("(cc p) d -> p cc d", p=128)[:, :, sl[0]:sl[1]],
                (wreg,))
            dma("D2", wsl(h, 1024, 512).rearrange("p (cc j) -> p cc j", cc=4),
                wuk_d.rearrange("(cc p) d -> p cc d", p=128)[:, :, sl[0]:sl[1]],
                (wreg,))
            dma("D2", wsl(h, 1536, 512).rearrange("p (cc j) -> p cc j", cc=4),
                wuv_d.rearrange("(cc p) d -> p cc d", p=128)[:, :, sl[0]:sl[1]],
                (wreg,))
            dma("D2", wsl(h, 2048, 1024).rearrange("p (cc j) -> p cc j", cc=8),
                wkr_d.rearrange("(cc p) d -> p cc d", p=128)[:, :, sl[0]:sl[1]],
                (wreg,))
            dma("D2", RTS[:, :], rts_d[c], ("RTS",))

        def load_l1_slices(c):
            h = c % 2
            wreg = "WSL%d" % h
            wre = wqkv_d.rearrange("(dc p) n -> p dc n", p=128)
            for k, off in ((0, 0), (1, 1024), (2, 2048)):
                dma("D2", wsl(h, k * 1024, 1024).rearrange(
                    "p (dc j) -> p dc j", dc=8),
                    wre[:, :, off + c * 128:off + c * 128 + 128], (wreg,))

        def jit_proj_fm(c, wcol, nchunk, srcT, sreg, bcol, dst, dreg):
            h = c % 2
            wreg = "WSL%d" % h
            for fs in range(4):
                qreg = ("PSa0", "PSb0", "PSa1", "PSb1")[fs]
                pslc = (PSa[:, 0:512], PSb[:, 0:512],
                        PSa[:, 512:1024], PSb[:, 512:1024])[fs]

                def tb(te, fs=fs, pslc=pslc, h=h, wcol=wcol, nchunk=nchunk,
                       srcT=srcT):
                    last = None
                    for cc in range(nchunk):
                        last = te.matmul(
                            pslc,
                            wsl(h, wcol + cc * 128, 128),
                            srcT[:, cc * 2048 + fs * 512:
                                 cc * 2048 + fs * 512 + 512],
                            start=(cc == 0), stop=(cc == nchunk - 1))
                    return last
                tgrp((wreg, sreg), (qreg,), tb)
                if fs % 2 == 0:
                    scgrp((qreg, "REP"), (dreg,), lambda sc, fs=fs,
                          pslc=pslc, bcol=bcol, dst=dst: sc.activation(
                              dst[:, fs * 512:fs * 512 + 512], pslc,
                              AF.Identity,
                              bias=REP[:, bcol + c:bcol + c + 1]))
                else:
                    vgrp((qreg, "REP"), (dreg,), lambda v, fs=fs, pslc=pslc,
                         bcol=bcol, dst=dst: v.tensor_scalar(
                             dst[:, fs * 512:fs * 512 + 512], pslc,
                             REP[:, bcol + c:bcol + c + 1], None, ALU.add))


        def jit_v_l0(c):
            h = c % 2
            wreg = "WSL%d" % h
            for half in range(2):
                preg = "PSa" if half == 0 else "PSb"
                phalf = PSa if half == 0 else PSb

                def tb(te, half=half, phalf=phalf, h=h):
                    last = None
                    for tt in range(8):
                        t = half * 8 + tt
                        o = phalf[:, tt * 128:tt * 128 + 128]
                        last = te.matmul(
                            o, ONER[0:1, 0:128],
                            BROW[0:1, B_UV + c * 128:B_UV + c * 128 + 128],
                            start=True, stop=False)
                        for cc in range(4):
                            last = te.matmul(
                                o,
                                CTK[:, cc * 2048 + t * 128:
                                    cc * 2048 + t * 128 + 128],
                                wsl(h, 1536 + cc * 128, 128),
                                start=False, stop=(cc == 3))
                    return last
                tgrp(("CTK", wreg, "ONES", "BROW"), (preg,), tb)

                def vb(v, half=half, phalf=phalf):
                    dst = VP[:, :].rearrange(
                        "p (kb a u) -> p kb a u", kb=16, a=2)[
                        :, half * 8:half * 8 + 8, :, 0:64]
                    return v.tensor_copy(
                        dst, phalf[:, 0:1024].rearrange(
                            "p (t a s) -> p t a s", t=8, a=2))
                vgrp((preg,), ("VP",), vb)

        def rope_fm(c, dst, dreg):
            def tbs(te):
                last = None
                for fs in range(4):
                    last = te.matmul(PS[:, fs * 512:fs * 512 + 512],
                                     SWP[:, :],
                                     CT[:, fs * 512:fs * 512 + 512],
                                     start=True, stop=True)
                return last
            tgrp(("CTr", "SWP"), ("PSa", "PSb"), tbs)

            def vbr(v):
                v.tensor_tensor(RB[:, :], PS[:, :], RTS[:, :], ALU.mult)
                v.drain()
                v.tensor_tensor(dst, dst, CT[:, :], ALU.add)
                v.drain()
                return v.tensor_tensor(dst, dst, RB[:, :], ALU.add)
            vgrp(("PSa", "PSb", "RTS", "CTr", dreg), 
                 (dreg, "RB0", "RB1"), vbr)

            if c == 0:
                def vb0(v):
                    for r in (0, 32):
                        v.tensor_tensor(RB[r:r + 1, 0:2048],
                                        CT[r:r + 1, :],
                                        RC0[r:r + 1, :], ALU.mult)
                    v.drain()
                    last = None
                    for r in (0, 32):
                        last = v.tensor_tensor(dst[r:r + 1, :],
                                               dst[r:r + 1, :],
                                               RB[r:r + 1, 0:2048], ALU.add)
                    return last
                vgrp(("CTr", "F1", dreg), (dreg, "RB0", "RB1"), vb0)


        def nb_square(which, buf, breg):
            off = which * 2048
            pr = ("PB0", "PB1") if which == 0 else ("PB2", "PB3")
            vgrp((breg,), pr, lambda v, buf=buf, off=off:
                 v.tensor_tensor(PB[:, off:off + 2048], buf[:, :],
                                 buf[:, :], ALU.mult))

        def nb_reduce(which):
            off = which * 2048
            prr = ("PB0", "PB1") if which == 0 else ("PB2", "PB3")
            ph = PSa if which == 0 else PSb
            pr = ("PSa0", "PSa1") if which == 0 else ("PSb0", "PSb1")

            def tb(te, ph=ph, off=off):
                last = None
                for s4 in range(4):
                    last = te.matmul(
                        ph[0:1, (s4 % 2) * 512:(s4 % 2) * 512 + 512],
                        ONEC[:, :],
                        PB[:, off + s4 * 512:off + s4 * 512 + 512],
                        start=(s4 < 2), stop=(s4 >= 2))
                return last
            tgrp(prr + ("ONES",), pr, tb)
            mcol = 17 + which
            vgrp(pr, ("STATn",), lambda v, ph=ph, mcol=mcol:
                 v.reduce_max(STAT[0:1, mcol:mcol + 1], ph[0:1, 0:1024],
                              axis=AX.X), drain=True)

        def nb_combine(scale):
            def vb(v):
                v.tensor_tensor(STAT[0:1, 21:22], STAT[0:1, 17:18],
                                STAT[0:1, 18:19], ALU.add)
                v.drain()
                v.tensor_scalar(STAT[0:1, 21:22], STAT[0:1, 21:22],
                                -0.5 * scale, None, ALU.mult)
                v.drain()
                return v.tensor_copy(MB[0:1, 1:2], STAT[0:1, 21:22])
            vgrp(("STATn",), ("STATn", "MB"), vb)
            tgrp(("MB", "ONES"), ("PSa0",), lambda te:
                 te.matmul(PSa[:, 40:41], ONER[0:1, 0:128], MB[0:1, 1:2],
                           start=True, stop=True))
            vgrp(("PSa0",), ("REP",), lambda v:
                 v.tensor_copy(REP[:, R_BA:R_BA + 1], PSa[:, 40:41]))

        pb_ctr = [0]

        def emit_attention(c, layer):
            scale = L0_SCALE if layer == 0 else L1_SCALE
            causal = (layer == 1)
            aoc = (c % 4) * 2048
            nb_combine(scale)
            for qh in range(2):
                q0 = qh * 1024
                kb_list = list(range(NKB)) if not causal else \
                    list(range(0, qh * 8 + 8))
                subs = []
                for j, kb in enumerate(kb_list):
                    lo = 0 if not causal else max(0, kb * 128 - q0)
                    for qs in range(2):
                        a0 = max(lo, qs * 512)
                        b0 = (qs + 1) * 512
                        if a0 >= b0:
                            continue
                        msk = (causal and kb * 128 >= q0 and
                               qs * 512 <= lo < (qs + 1) * 512)
                        subs.append([kb, qs, a0, b0, j == 0,
                                     j == len(kb_list) - 1, msk, lo])

                def emit_qk_exp(sub):
                    kb, qs, a0, b0, first, last_kb, msk, lo = sub
                    pb = pb_ctr[0] % 4
                    pb_ctr[0] += 1
                    sub.append(pb)
                    pbreg = "PB%d" % pb
                    psq = ("PSa%d" % qs, "PSb%d" % qs)

                    def tb(te, kb=kb, a0=a0, b0=b0, q0=q0):
                        last = None
                        for (rb, psx) in ((0, PSa), (64, PSb)):
                            last = te.matmul(
                                psx[:, a0:b0],
                                KF[rb:rb + 64, kb * 128:kb * 128 + 128],
                                QF[rb:rb + 64, q0 + a0:q0 + b0],
                                start=True, stop=True,
                                tile_position=(rb, 0))
                        return last
                    tgrp(("QF", "KF"), psq, tb)

                    def scb(sc, a0=a0, b0=b0, pb=pb, qs=qs):
                        i_ap = PS[:, :].rearrange(
                            "p (h q) -> p h q", h=2)[:, :, a0:b0]
                        o_ap = PB[:, pb * 1024:pb * 1024 + 1024].rearrange(
                            "p (h q) -> p h q", h=2)[
                            :, :, a0 - qs * 512:b0 - qs * 512]
                        return sc.activation(o_ap, i_ap, AF.Exp,
                                             bias=REP[:, R_BA:R_BA + 1],
                                             scale=scale)
                    scgrp(psq + ("REP",), (pbreg,), scb)

                    if msk:
                        def vb(v, lo=lo, pb=pb, qs=qs):
                            ll = lo - qs * 512
                            pA = PB[:, pb * 1024 + ll:pb * 1024 + ll + 128]
                            pB = PB[:, pb * 1024 + 512 + ll:
                                    pb * 1024 + 512 + ll + 128]
                            v.tensor_tensor(pA, pA, CMK[:, :], ALU.mult)
                            return v.tensor_tensor(pB, pB, CMK[:, :],
                                                   ALU.mult)
                        vgrp((pbreg, "CMK"), (pbreg,), vb)

                def emit_pv(sub):
                    kb, qs, a0, b0, first, last_kb, msk, lo, pb = sub
                    pbreg = "PB%d" % pb

                    def tb2(te, kb=kb, a0=a0, b0=b0, pb=pb, qs=qs,
                            first=first, last_kb=last_kb):
                        last = None
                        for hh, pv in ((0, PVA), (1, PVB)):
                            last = te.matmul(
                                pv[:, a0:b0],
                                VP[:, kb * 130 + hh * 65:
                                   kb * 130 + hh * 65 + 65],
                                PB[:, pb * 1024 + hh * 512 + a0 - qs * 512:
                                   pb * 1024 + hh * 512 + b0 - qs * 512],
                                start=first, stop=last_kb,
                                skip_group_check=True)
                        return last
                    tgrp((pbreg, "VP"), ("PVA", "PVB"), tb2)

                for i, sub in enumerate(subs):
                    emit_qk_exp(sub)
                    if i >= 2:
                        emit_pv(subs[i - 2])
                emit_pv(subs[-2])
                emit_pv(subs[-1])

                rdA = F1[0:1, 0:1024]
                rdB = F1[64:65, 0:1024]
                PBF = PB[:, :].bitcast(F32)

                def scl(sc, rdA=rdA, rdB=rdB):
                    sc.activation(rdA, PVA[64:65, 0:1024], AF.Ln)
                    return sc.activation(rdB, PVB[64:65, 0:1024], AF.Ln)
                scgrp(("PVA", "PVB"), ("F1",), scl)

                def tb3(te, rdA=rdA, rdB=rdB):
                    last = None
                    for fs in range(2):
                        last = te.matmul(
                            PSa[0:64, fs * 512:fs * 512 + 512],
                            ONERF[0:1, :],
                            rdA[:, fs * 512:fs * 512 + 512],
                            start=True, stop=True)
                        last = te.matmul(
                            PSb[0:64, fs * 512:fs * 512 + 512],
                            ONERF[64:65, :],
                            rdB[:, fs * 512:fs * 512 + 512],
                            start=True, stop=True)
                    return last
                tgrp(("F1", "ONES"), ("PSa", "PSb"), tb3)

                scgrp(("PSa", "PSb"), ("PB0", "PB1"), lambda sc, PBF=PBF: (
                    sc.activation(PBF[0:64, 0:1024], PSa[0:64, 0:1024],
                                  AF.Exp, scale=-1.0),
                    sc.activation(PBF[0:64, 1024:2048], PSb[0:64, 0:1024],
                                  AF.Exp, scale=-1.0))[-1])

                def vb2(v, qh=qh, aoc=aoc, PBF=PBF):
                    v.tensor_tensor(AO[0:64, aoc + qh * 1024:
                                       aoc + qh * 1024 + 1024],
                                    PVA[0:64, 0:1024], PBF[0:64, 0:1024],
                                    ALU.mult)
                    return v.tensor_tensor(
                        CT[0:64, qh * 1024:qh * 1024 + 1024],
                        PVB[0:64, 0:1024], PBF[0:64, 1024:2048], ALU.mult)
                vgrp(("PVA", "PVB", "PB0", "PB1"), ("AO", "CTr"), vb2)

            def tb4(te):
                last = None
                for fs in range(4):
                    last = te.matmul(
                        PS[64:128, fs * 512:fs * 512 + 512],
                        IDE[0:64, 0:64],
                        CT[0:64, fs * 512:fs * 512 + 512],
                        start=True, stop=True, tile_position=(0, 64))
                return last
            tgrp(("CTr", "IDE"), ("PSa", "PSb"), tb4)
            vgrp(("PSa", "PSb"), ("AO",), lambda v, aoc=aoc:
                 v.tensor_copy(AO[64:128, aoc:aoc + 2048], PS[64:128, :]))

        def emit_l0_pair(c):
            if c + 1 < NP:
                load_l0_slices(c + 1)
            elif BUILD >= 3:
                load_l1_slices(0)
            jit_proj_fm(c, 512, 4, CTQ, "CTQ", R_BUQ, QF, "QF")
            jit_proj_fm(c, 0, 4, CTQ, "CTQ", R_BQR, CT, "CTr")
            rope_fm(c, QF[:, :], "QF")
            nb_square(0, QF, "QF")
            jit_proj_fm(c, 1024, 4, CTK, "CTK", R_BUK, KF, "KF")
            jit_proj_fm(c, 2048, 8, XNT, "XNT", R_BKR, CT, "CTr")
            rope_fm(c, KF[:, :], "KF")
            nb_square(1, KF, "KF")
            jit_v_l0(c)
            nb_reduce(0)
            nb_reduce(1)
            emit_attention(c, 0)

        def emit_l1_pair(c):
            if c + 1 < NP:
                load_l1_slices(c + 1)
            h = c % 2
            wreg = "WSL%d" % h
            for which, dst, dreg in ((0, QF, "QF"), (1, KF, "KF")):
                def emit_qk(which=which, dst=dst, dreg=dreg):
                    for fs in range(4):
                        half = fs % 2
                        preg = "PSa" if half == 0 else "PSb"
                        pslc = (PSa if half == 0 else PSb)[:, 0:512]

                        def tb2(te, which=which, fs=fs, pslc=pslc, h=h):
                            last = None
                            for dc in range(NCH):
                                last = te.matmul(
                                    pslc,
                                    wsl(h, which * 1024 + dc * 128, 128),
                                    XNT[:, dc * 2048 + fs * 512:
                                        dc * 2048 + fs * 512 + 512],
                                    start=(dc == 0), stop=False)
                            return te.matmul(
                                pslc,
                                BROW[0:1, B_BQK1 + which * 1024 + c * 128:
                                     B_BQK1 + which * 1024 + c * 128 + 128],
                                ONER[0:1, 0:512],
                                start=False, stop=True)
                        tgrp((wreg, "XNT", "BROW", "ONES"), (preg,), tb2)
                        evac(preg, dreg, dst[:, fs * 512:fs * 512 + 512],
                             pslc, fs % 2 == 0)
                emit_qk()
                nb_square(which, dst, dreg)
            for half in range(2):
                preg = "PSa" if half == 0 else "PSb"
                phalf = PSa if half == 0 else PSb

                def tb(te, half=half, phalf=phalf, h=h):
                    last = None
                    for tt in range(8):
                        t = half * 8 + tt
                        o = phalf[:, tt * 128:tt * 128 + 128]
                        last = te.matmul(
                            o, ONER[0:1, 0:128],
                            BROW[0:1, B_V1 + c * 128:B_V1 + c * 128 + 128],
                            start=True, stop=False)
                        for dc in range(NCH):
                            last = te.matmul(
                                o,
                                XNT[:, dc * 2048 + t * 128:
                                    dc * 2048 + t * 128 + 128],
                                wsl(h, 2048 + dc * 128, 128),
                                start=False, stop=(dc == NCH - 1))
                    return last
                tgrp(("XNT", wreg, "ONES", "BROW"), (preg,), tb)

                def vb(v, half=half, phalf=phalf):
                    dst = VP[:, :].rearrange(
                        "p (kb a u) -> p kb a u", kb=16, a=2)[
                        :, half * 8:half * 8 + 8, :, 0:64]
                    return v.tensor_copy(
                        dst, phalf[:, 0:1024].rearrange(
                            "p (t a s) -> p t a s", t=8, a=2))
                vgrp((preg,), ("VP",), vb)
            nb_reduce(0)
            nb_reduce(1)
            emit_attention(c, 1)

        def emit_wo(layer, halfgrp):
            wo_d = wo0_d if layer == 0 else wo1_d
            wor = wo_d.rearrange("(ch p) n -> p ch n", p=128)
            for c4 in range(4):
                ch = halfgrp * 4 + c4
                wreg = "PB%d" % c4
                dma("D2", PB[:, c4 * 1024:c4 * 1024 + 1024],
                    wor[:, ch, :], (wreg,))
            for t in range(NT):
                half = t % 2
                preg = "PSa" if half == 0 else "PSb"
                phalf = PSa if half == 0 else PSb

                def tb(te, t=t, phalf=phalf, halfgrp=halfgrp):
                    last = None
                    if halfgrp == 1:
                        for fs in range(2):
                            last = te.matmul(
                                phalf[:, fs * 512:fs * 512 + 512],
                                ONER[0:1, 0:128],
                                BROW[0:1, B_WO + fs * 512:
                                     B_WO + fs * 512 + 512],
                                start=True, stop=False)
                    for c4 in range(4):
                        for fs in range(2):
                            last = te.matmul(
                                phalf[:, fs * 512:fs * 512 + 512],
                                AO[:, c4 * 2048 + t * 128:
                                   c4 * 2048 + t * 128 + 128],
                                PB[:, c4 * 1024 + fs * 512:
                                   c4 * 1024 + fs * 512 + 512],
                                start=(c4 == 0 and halfgrp == 0),
                                stop=(c4 == 3))
                    return last
                tgrp(("AO", "PB0", "PB1", "PB2", "PB3", "ONES", "BROW"),
                     (preg,), tb)
                vgrp((preg, "X"), ("X",), lambda v, t=t, phalf=phalf:
                     v.tensor_tensor(X[:, t * D:(t + 1) * D],
                                     X[:, t * D:(t + 1) * D],
                                     phalf[:, 0:1024], ALU.add))

        def emit_l0_ffn():
            XINT = CTK[:, 0:2048].bitcast(I32)
            F2 = CTK[:, 2048:4096].bitcast(F32)
            F3 = CTK[:, 4096:6144].bitcast(F32)
            TSTG = CTK[:, 6144:8192].bitcast(I32)
            WT = CTQ[:, :]
            FBR = QF[:, 0:2048].bitcast(F32)

            dma("D2", FBR, fbr_d, ("QF",))
            def vb(v):
                v.tensor_scalar_max(STAT[0:1, 22:23], STAT[0:1, 8:9], 1e-5)
                v.drain()
                v.reciprocal(STAT[0:1, 23:24], STAT[0:1, 22:23])
                v.drain()
                v.tensor_copy(MB[0:1, 0:1], STAT[0:1, 22:23])
                return v.tensor_copy(MB[0:1, 1:2], STAT[0:1, 23:24])
            vgrp(("STATf",), ("STATf", "MB"), vb)
            def vb2(v):
                v.tensor_tensor(STAT[0:1, 24:25], STAT[0:1, 22:23],
                                MB[0:1, 0:1], ALU.subtract)
                v.tensor_tensor(STAT[0:1, 25:26], STAT[0:1, 23:24],
                                MB[0:1, 1:2], ALU.subtract)
                v.drain()
                v.tensor_copy(PB[0:1, 0:1], STAT[0:1, 24:25])
                return v.tensor_copy(PB[0:1, 1:2], STAT[0:1, 25:26])
            vgrp(("STATf", "MB"), ("STATf", "PB0"), vb2)

            def tb(te):
                te.matmul(PSa[:, 0:1], ONER[0:1, 0:128], MB[0:1, 0:1],
                          start=True, stop=False)
                te.matmul(PSa[:, 0:1], ONER[0:1, 0:128], PB[0:1, 0:1],
                          start=False, stop=True)
                te.matmul(PSa[:, 1:2], ONER[0:1, 0:128], MB[0:1, 1:2],
                          start=True, stop=False)
                return te.matmul(PSa[:, 1:2], ONER[0:1, 0:128], PB[0:1, 1:2],
                                 start=False, stop=True)
            tgrp(("MB", "PB0", "ONES"), ("PSa",), tb)
            vgrp(("PSa",), ("REP",), lambda v: (
                v.tensor_copy(REP[:, R_SREP:R_SREP + 1], PSa[:, 0:1]),
                v.tensor_copy(REP[:, R_RCP:R_RCP + 1], PSa[:, 1:2]))[-1])

            twr = tw_d.rearrange("(ob p) i -> p ob i", p=128)
            for ob in range(NCH):
                dma("D1", TSTG, twr[:, ob, :], ("CTK",))
                scgrp(("CTK", "CONSTF"), ("PB0",), lambda sc:
                      sc.activation(PB[:, 0:1024], TSTG, AF.Identity,
                                    bias=CONSTF[:, 1:2]))
                for g4 in range(2):
                    preg = "PSa" if g4 == 0 else "PSb"
                    pt = (PSa if g4 == 0 else PSb)[:, 0:256].bitcast(BF16)

                    def tb2(te, g4=g4, pt=pt):
                        last = None
                        for j in range(4):
                            ci = g4 * 4 + j
                            last = te.transpose(
                                pt[:, j * 128:j * 128 + 128],
                                PB[:, ci * 128:ci * 128 + 128], IDE[:, :])
                        return last
                    tgrp(("PB0", "IDE"), (preg,), tb2)

                    def vb3(v, ob=ob, g4=g4, pt=pt):
                        dst = WT.rearrange("p (ci u) -> p ci u", ci=NCH)[
                            :, g4 * 4:g4 * 4 + 4, ob * 128:ob * 128 + 128]
                        return v.tensor_copy(
                            dst, pt.rearrange("p (ci u) -> p ci u", ci=4))
                    vgrp((preg,), ("CTQ",), vb3)

            dbg("ffnrep", REP[:, :], (128, 64), ("REP",))
            dbg("ffnwt", CTQ[:, :], (128, 8192), ("CTQ",))
            emit_ln_stats_all()
            for t in range(NT):
                ln_apply(t, F1[:, :], "F1")

                def vb4(v, t=t):
                    v.tensor_scalar(F2[:, 0:1024], F1[:, :],
                                    REP[:, R_RCP:R_RCP + 1], 127.0,
                                    ALU.mult, ALU.min)
                    v.drain()
                    v.tensor_scalar_max(F3[:, 0:1024], F2[:, 0:1024], -127.0)
                    v.drain()
                    v.tensor_copy(XINT[:, 0:1024], F3[:, 0:1024])
                    v.drain()
                    lreg = RB[:, (t % 2) * D:(t % 2) * D + D]
                    return v.tensor_copy(lreg, XINT[:, 0:1024])
                vgrp(("CTK", "F1", "REP"), ("CTK", "RB%d" % (t % 2)), vb4)
                for half in range(2):
                    preg = "PSa" if half == 0 else "PSb"
                    pt = (PSa if half == 0 else PSb)[:, 0:256].bitcast(BF16)

                    def tb3(te, t=t, half=half, pt=pt):
                        last = None
                        for cc in range(4):
                            c = half * 4 + cc
                            last = te.transpose(
                                pt[:, cc * 128:cc * 128 + 128],
                                RB[:, (t % 2) * D + c * 128:
                                   (t % 2) * D + c * 128 + 128], IDE[:, :])
                        return last
                    tgrp(("RB%d" % (t % 2), "IDE"), (preg,), tb3)

                    def vb5(v, t=t, half=half, pt=pt):
                        dst = XNT[:, :].rearrange("p (c s) -> p c s", c=NCH)[
                            :, half * 4:half * 4 + 4, t * 128:t * 128 + 128]
                        return v.tensor_copy(
                            dst, pt.rearrange("p (c s) -> p c s", c=4))
                    vgrp((preg,), ("XNT",), vb5)
                half = t % 2
                preg = "PSa" if half == 0 else "PSb"
                phalf = PSa if half == 0 else PSb

                def tb4(te, t=t, phalf=phalf):
                    last = None
                    for fs in range(2):
                        for ci in range(NCH):
                            last = te.matmul(
                                phalf[:, fs * 512:fs * 512 + 512],
                                XNT[:, ci * 2048 + t * 128:
                                    ci * 2048 + t * 128 + 128],
                                WT[:, ci * 1024 + fs * 512:
                                   ci * 1024 + fs * 512 + 512],
                                start=(ci == 0), stop=(ci == NCH - 1))
                    return last
                tgrp(("XNT", "CTQ"), (preg,), tb4)

                def vb6(v, t=t, phalf=phalf):
                    v.scalar_tensor_tensor(F1[:, :], phalf[:, 0:1024],
                                           REP[:, R_SREP:R_SREP + 1],
                                           FBR[:, 0:1024],
                                           ALU.mult, ALU.add)
                    v.drain()
                    return v.tensor_tensor(X[:, t * D:(t + 1) * D],
                                           X[:, t * D:(t + 1) * D],
                                           F1[:, :], ALU.add)
                vgrp((preg, "REP", "QF", "X", "F1"), ("F1", "X"), vb6)
                if t == 0:
                    dbg("ffnxq", XNT[:, :], (128, NCH * S), ("XNT",))
                    dbg("ffnq0", RB[:, :], (128, 2048), ("RB0", "RB1"))

        def emit_l1_ffn():
            ST = CTQ[0:64, 0:2048]
            EMS = CTQ[0:64, 2048:3072]
            W2E = CTQ[:, 3072:3584]
            WINS = CTQ[:, 3584:4096]
            WOUTS = CTQ[0:64, 4096:5120]
            GT = CTQ[0:64, 5120:7168]
            GTOK = CTQ[:, 7168:8192]
            E2S = CTK[:, 0:512]

            dma("D2", WINS.rearrange("p (dc m) -> p dc m", dc=NCH),
                win_d.rearrange("(dc p) m -> p dc m", p=128), ("CTQ",))
            dma("D2", EMS, emat_d, ("CTQ",))
            dma("D2", E2S.rearrange("p (ci m) -> p ci m", ci=NCH),
                e2_d.rearrange("(ci p) m -> p ci m", p=128), ("CTK",))
            dma("D2", STAT[0:64, 28:29], binc_d, ("STATb",))
            dma("D2", REP[:, R_W1C:R_W1C + 8], w1c_d, ("REP",))
            dma("D2", REP[:, R_B1C:R_B1C + 8], b1c_d, ("REP",))
            dma("D2", REP[:, R_W2C:R_W2C + 8], w2c_d, ("REP",))
            dma("D2", BROW[:, B_FO:B_FO + D + 64], browf_d, ("BROW",))
            dma("D2", WOUTS, wout1_d, ("CTQ",))

            def vb(v):
                last = None
                for ci in range(NCH):
                    last = v.tensor_scalar(
                        W2E[:, ci * 64:ci * 64 + 64],
                        E2S[:, ci * 64:ci * 64 + 64],
                        REP[:, R_W2C + ci:R_W2C + ci + 1], None, ALU.mult)
                return last
            vgrp(("CTK", "REP"), ("CTQ",), vb)

            emit_ln_to_xnt()

            for fs in range(4):
                half = fs % 2
                preg = "PSa" if half == 0 else "PSb"
                pslc = (PSa if half == 0 else PSb)[0:64, 0:512]

                def tb(te, fs=fs, pslc=pslc):
                    last = None
                    for dc in range(NCH):
                        last = te.matmul(
                            pslc, WINS[:, dc * 64:dc * 64 + 64],
                            XNT[:, dc * 2048 + fs * 512:
                                dc * 2048 + fs * 512 + 512],
                            start=(dc == 0), stop=(dc == NCH - 1))
                    return last
                tgrp(("CTQ", "XNT"), (preg,), tb)
                scgrp((preg, "STATb"), ("CTQ",), lambda sc, fs=fs, pslc=pslc:
                      sc.activation(ST[:, fs * 512:fs * 512 + 512], pslc,
                                    AF.Identity, bias=STAT[0:64, 28:29]))

            for ci in range(NCH):
                def tb2(te, ci=ci):
                    last = None
                    for fs in range(4):
                        last = te.matmul(
                            PS[:, fs * 512:fs * 512 + 512],
                            EMS[:, ci * 128:ci * 128 + 128],
                            ST[:, fs * 512:fs * 512 + 512],
                            start=True, stop=True)
                    return last
                tgrp(("CTQ",), ("PSa", "PSb"), tb2)
                scgrp(("PSa", "PSb", "REP"), ("XNT",), lambda sc, ci=ci:
                      sc.activation(XNT[:, ci * 2048:ci * 2048 + 2048],
                                    PS[:, :], AF.Relu,
                                    scale=REP[:, R_W1C + ci:R_W1C + ci + 1],
                                    bias=REP[:, R_B1C + ci:R_B1C + ci + 1]))

            for half in range(2):
                preg = "PSa" if half == 0 else "PSb"
                phalf = PSa if half == 0 else PSb

                def tb3(te, half=half, phalf=phalf):
                    last = None
                    for tt in range(8):
                        t = half * 8 + tt
                        o = phalf[:, tt * 64:tt * 64 + 64]
                        last = te.matmul(o, ONER[0:1, 0:128],
                                         BROW[0:1, B_B2:B_B2 + 64],
                                         start=True, stop=False)
                        for ci in range(NCH):
                            last = te.matmul(
                                o,
                                XNT[:, ci * 2048 + t * 128:
                                    ci * 2048 + t * 128 + 128],
                                W2E[:, ci * 64:ci * 64 + 64],
                                start=False, stop=(ci == NCH - 1))
                    return last
                tgrp(("XNT", "CTQ", "ONES", "BROW"), (preg,), tb3)
                vgrp((preg,), ("CTQ",), lambda v, half=half, phalf=phalf:
                     v.tensor_copy(GTOK[:, half * 512:half * 512 + 512],
                                   phalf[:, 0:512]))
            for g4 in range(4):
                half = g4 % 2
                preg = "PSa" if half == 0 else "PSb"
                pt = (PSa if half == 0 else PSb)[0:64, 0:256].bitcast(BF16)

                def tb4(te, g4=g4, pt=pt):
                    last = None
                    for j in range(4):
                        t = g4 * 4 + j
                        last = te.transpose(pt[:, j * 128:j * 128 + 128],
                                            GTOK[:, t * 64:t * 64 + 64],
                                            IDE[:, :])
                    return last
                tgrp(("CTQ", "IDE"), (preg,), tb4)
                vgrp((preg,), ("CTQ",), lambda v, g4=g4, pt=pt:
                     v.tensor_copy(GT[:, g4 * 512:g4 * 512 + 512], pt))

            for t in range(NT):
                half = t % 2
                preg = "PSa" if half == 0 else "PSb"
                phalf = PSa if half == 0 else PSb

                def tb5(te, t=t, phalf=phalf):
                    last = None
                    for fs in range(2):
                        o = phalf[:, fs * 512:fs * 512 + 512]
                        last = te.matmul(o, ONER[0:1, 0:128],
                                         BROW[0:1, B_FO + fs * 512:
                                              B_FO + fs * 512 + 512],
                                         start=True, stop=False)
                        last = te.matmul(
                            o, GT[:, t * 128:t * 128 + 128],
                            WOUTS[:, fs * 512:fs * 512 + 512],
                            start=False, stop=True)
                    return last
                tgrp(("CTQ", "ONES", "BROW"), (preg,), tb5)
                vgrp((preg, "X"), ("X",), lambda v, t=t, phalf=phalf:
                     v.tensor_tensor(X[:, t * D:(t + 1) * D],
                                     X[:, t * D:(t + 1) * D],
                                     phalf[:, 0:1024], ALU.add))

        load_l0_slices(0)
        emit_ln_to_xnt()
        dbg("xnt", XNT[:, :], (128, NCH * S), ("XNT",))
        emit_ct()
        dbg("ctq", CTQ[:, :], (128, 8192), ("CTQ",))
        dbg("ctk", CTK[:, :], (128, 8192), ("CTK",))
        for c in range(NP):
            emit_l0_pair(c)
            if c == 0 and dbg_on:
                dbg("qf0", QF[:, :], (128, 2048), ("QF",))
                dbg("kf0", KF[:, :], (128, 2048), ("KF",))
                dbg("vp0", VP[:, :], (128, 2080), ("VP",))
                dbg("rep0", REP[:, :], (128, 64), ("REP",))
                dbg("f1r", F1[:, :], (128, 1024), ("F1",))
                dbg("ao0", AO[:, 0:2048], (128, 2048), ("AO",))
            if c == 3:
                if dbg_on:
                    dbg("aoh0", AO[:, :], (128, 8192), ("AO",))
                emit_wo(0, 0)
            if c == 7 and dbg_on:
                dbg("aoh1", AO[:, :], (128, 8192), ("AO",))
        emit_wo(0, 1)
        dbg("xfin", X[:, :], (128, NT * D), ("X",))

        if BUILD >= 2:
            emit_l0_ffn()

        if BUILD >= 3:
            dma("D2", BROW[:, B_V1:B_V1 + D], bv1_d, ("BROW",))
            dma("D2", BROW[:, B_WO:B_WO + D], bwo1_d, ("BROW",))
            dma("D2", BROW[:, B_BQK1:B_BQK1 + 2 * D], bqk_d, ("BROW",))
            emit_ln_to_xnt()
            for c in range(NP):
                emit_l1_pair(c)
                if c == 3:
                    emit_wo(1, 0)
            emit_wo(1, 1)

        if BUILD >= 4:
            emit_l1_ffn()

        pend = []
        res.rd(pend, "D1", "X")
        final_tok = sem.reserve("D1", 16)

        def final_store(d1, pend=pend, final_tok=final_tok):
            for s, t in pend:
                d1.wait_ge(sem.h[s], t)
            i = d1.dma_start(out_d.rearrange("(t p) d -> p t d", p=128),
                             X[:, :].rearrange("p (t d) -> p t d", t=NT))
            i.then_inc(sem.h["D1"], 16)
            d1.wait_ge(sem.h["D1"], final_tok)
            return i
        STREAMS["D1"].append(([], final_store, [], False))

        flush_streams()

    return nc


_NC_CACHE = {}
DBG_OUT = {}


def _theta():
    ids = np.arange(512, dtype=np.float64)
    return (10000.0 ** (-2.0 * ids) / 512.0)


def _perm():
    p = []
    for h in range(16):
        p.extend(range(h * 64, h * 64 + 64, 2))
        p.extend(range(h * 64 + 1, h * 64 + 64, 2))
    return np.array(p)


def kernel(L0_attn_x, L0_attn_Wd_q_w, L0_attn_Wd_q_b, L0_attn_Wu_q_w,
           L0_attn_Wu_q_b, L0_attn_Wq_r_w, L0_attn_Wq_r_b, L0_attn_Wk_r_w,
           L0_attn_Wk_r_b, L0_attn_Wd_kv_w, L0_attn_Wd_kv_b, L0_attn_Wu_k_w,
           L0_attn_Wu_k_b, L0_attn_Wu_v_w, L0_attn_Wu_v_b, L0_attn_Wo_w,
           L0_attn_Wo_b, L0_ffn_ternary_weight, L0_ffn_bias, L0_ffn_act_scale,
           L1_attn_W_qkv, L1_attn_b_qkv, L1_attn_W_out, L1_attn_b_out,
           L1_ffn_Win, L1_ffn_bin_, L1_ffn_W1, L1_ffn_b1, L1_ffn_W2,
           L1_ffn_b2, L1_ffn_Wout, L1_ffn_bout):
    f32 = np.float32
    perm = _perm()

    def c(a):
        return np.ascontiguousarray(np.asarray(a), dtype=f32)

    theta = _theta()
    pos = np.arange(S, dtype=np.float64)
    rts = np.empty((NP, 128, S), dtype=BF)
    r_idx = np.arange(128)
    hd, kk, ii = r_idx // 64, (r_idx % 64) // 32, r_idx % 32
    sign = np.where(kk == 0, -1.0, 1.0)
    for cc in range(NP):
        th = theta[64 * cc + hd * 32 + ii]
        rts[cc] = (sign[:, None] *
                   np.sin(pos[None, :] * th[:, None])).astype(BF)
    rc0 = np.zeros((33, S), dtype=f32)
    c0m1 = np.cos(pos * theta[0]) - 1.0
    rc0[0, :] = c0m1
    rc0[32, :] = c0m1
    swp = np.zeros((128, 128), dtype=f32)
    jj = np.where((r_idx % 64) < 32, r_idx + 32, r_idx - 32)
    swp[r_idx, jj] = 1.0

    bqr = c(L0_attn_Wq_r_b)[perm]
    buq = c(L0_attn_Wu_q_b)[perm]
    bkr = c(L0_attn_Wk_r_b)[perm]
    buk = c(L0_attn_Wu_k_b)[perm]
    brow0 = np.concatenate([bqr, buq, bkr, buk, c(L0_attn_Wu_v_b)])[None, :]

    b_qkv = c(L1_attn_b_qkv)

    common = dict(
        wdq=c(L0_attn_Wd_q_w),
        wdkv=c(L0_attn_Wd_kv_w),
        wuq=np.ascontiguousarray(c(L0_attn_Wu_q_w)[:, perm]),
        wqr=np.ascontiguousarray(c(L0_attn_Wq_r_w)[:, perm]),
        wuk=np.ascontiguousarray(c(L0_attn_Wu_k_w)[:, perm]),
        wkr=np.ascontiguousarray(c(L0_attn_Wk_r_w)[:, perm]),
        wuv=c(L0_attn_Wu_v_w),
        wo0=c(L0_attn_Wo_w),
        bdq_t=np.ascontiguousarray(c(L0_attn_Wd_q_b).reshape(4, 128).T),
        bdkv_t=np.ascontiguousarray(c(L0_attn_Wd_kv_b).reshape(4, 128).T),
        brow0=np.ascontiguousarray(brow0),
        bwo0=c(L0_attn_Wo_b)[None, :],
        rts=rts,
        rc0=rc0,
        swp=swp,
        bqr_c=np.ascontiguousarray(bqr.reshape(8, 128).T),
        buq_c=np.ascontiguousarray(buq.reshape(8, 128).T),
        bkr_c=np.ascontiguousarray(bkr.reshape(8, 128).T),
        buk_c=np.ascontiguousarray(buk.reshape(8, 128).T),
        tw=np.ascontiguousarray(np.asarray(L0_ffn_ternary_weight,
                                           dtype=np.int32)),
        fb_rep=np.ascontiguousarray(np.broadcast_to(c(L0_ffn_bias)[None, :],
                                                    (128, D))),
        ascale=c(L0_ffn_act_scale).reshape(1, 1),
        wqkv=c(L1_attn_W_qkv),
        bqk_row=np.ascontiguousarray(b_qkv[0:2048][None, :]),
        bv1=np.ascontiguousarray(b_qkv[2048:3072][None, :]),
        wo1=c(L1_attn_W_out),
        bwo1=c(L1_attn_b_out)[None, :],
        win=c(L1_ffn_Win),
        bin_c=c(L1_ffn_bin_).reshape(64, 1),
        w1c=np.ascontiguousarray(c(L1_ffn_W1).reshape(1024).reshape(8, 128).T),
        b1c=np.ascontiguousarray(c(L1_ffn_b1).reshape(1024).reshape(8, 128).T),
        emat=np.ascontiguousarray(
            (np.arange(D)[None, :] // 16 == np.arange(64)[:, None])
            .astype(f32)),
        e2=np.ascontiguousarray(
            (np.arange(D)[:, None] // 16 == np.arange(64)[None, :])
            .astype(f32)),
        w2c=np.ascontiguousarray(c(L1_ffn_W2).reshape(1024).reshape(8, 128).T),
        browf=np.concatenate([c(L1_ffn_bout), c(L1_ffn_b2)])[None, :],
        wout1=c(L1_ffn_Wout),
        ident=np.eye(128, dtype=f32),
        cmask=np.ascontiguousarray(
            (np.arange(128)[None, :] >= np.arange(128)[:, None])
            .astype(f32)),
    )

    if "nc" not in _NC_CACHE:
        _NC_CACHE["nc"] = build_program()
    nc = _NC_CACHE["nc"]

    x = np.asarray(L0_attn_x, dtype=f32)
    in_maps = []
    for i in range(8):
        m = dict(common)
        m["x"] = np.ascontiguousarray(x[i])
        in_maps.append(m)
    r = run_bass_kernel_spmd(nc, in_maps, core_ids=list(range(8)))
    if os.environ.get("KDBG"):
        DBG_OUT.clear()
        DBG_OUT.update({k: np.asarray(v) for k, v in r.results[0].items()
                        if k.startswith("dbg_")})
    return np.stack([np.asarray(rr["out"]) for rr in r.results], axis=0)


# version 23 (final)
# speedup vs baseline: 1.0547x; 1.0018x over previous
import numpy as np
import ml_dtypes
from contextlib import ExitStack

import concourse.bass as bass
import concourse.mybir as mybir
from concourse.bass_utils import run_bass_kernel_spmd

F32 = mybir.dt.float32
BF16 = mybir.dt.bfloat16
I32 = mybir.dt.int32
AF = mybir.ActivationFunctionType
ALU = mybir.AluOpType
AX = mybir.AxisListType

S = 2048
D = 1024
NT = 16
NCH = 8
NP = 8
NKB = 16
DQ = 512
L0_SCALE = float(1.0 / np.sqrt(np.float64(16 + 512 + 512)))
L1_SCALE = 0.125
BF = ml_dtypes.bfloat16

import os
BUILD = int(os.environ.get("KBUILD", "4"))


class Sems:
    def __init__(self, ctx, nc, names):
        self.h = {}
        self.c = {}
        for n in names:
            self.h[n] = ctx.enter_context(nc.semaphore(n))
            self.c[n] = 0

    def reserve(self, name, n=1):
        self.c[name] += n
        return self.c[name]


class Res:

    def __init__(self, sems):
        self.sems = sems
        self.w = {}
        self.r = {}
        self.waited = {}

    def _wait(self, pend, ename, sname, tok, force=False):
        if sname == ename and not force:
            return
        wd = self.waited.setdefault(ename, {})
        if wd.get(sname, 0) >= tok:
            return
        pend.append((sname, tok))
        wd[sname] = tok

    def rd(self, pend, ename, *regions):
        for reg in regions:
            if reg in self.w:
                s, t = self.w[reg]
                self._wait(pend, ename, s, t)

    def wr(self, pend, ename, *regions):
        dma_q = ename in ("D1", "D2")
        for reg in regions:
            if reg in self.w:
                s, t = self.w[reg]
                self._wait(pend, ename, s, t, force=dma_q and s == ename)
            for s, t in self.r.get(reg, {}).items():
                self._wait(pend, ename, s, t)

    def did_rd(self, ename, tok, *regions):
        for reg in regions:
            self.r.setdefault(reg, {})[ename] = tok

    def did_wr(self, ename, tok, *regions):
        for reg in regions:
            self.w[reg] = (ename, tok)
            self.r[reg] = {}


def build_program():
    nc = bass.Bass("TRN2", target_bir_lowering=False, debug=False)

    def din(name, shape, dt=F32):
        return nc.dram_tensor(name, list(shape), dt, kind="ExternalInput").ap()

    x_in = din("x", (S, D))
    out_d = nc.dram_tensor("out", [S, D], F32, kind="ExternalOutput").ap()

    wdq_d = din("wdq", (D, DQ))
    wdkv_d = din("wdkv", (D, DQ))
    wuq_d = din("wuq", (DQ, D))
    wqr_d = din("wqr", (DQ, D))
    wuk_d = din("wuk", (DQ, D))
    wkr_d = din("wkr", (D, D))
    wuv_d = din("wuv", (DQ, D))
    wo0_d = din("wo0", (D, D))
    bdq_d = din("bdq_t", (128, 4))
    bdkv_d = din("bdkv_t", (128, 4))
    brow0_d = din("brow0", (1, 5 * D))
    bwo0_d = din("bwo0", (1, D))
    rts_d = din("rts", (NP, 128, 2048), BF16)
    rc0_d = din("rc0", (33, 2048))
    swp_d = din("swp", (128, 128))
    bqrc_d = din("bqr_c", (128, 8))
    buqc_d = din("buq_c", (128, 8))
    bkrc_d = din("bkr_c", (128, 8))
    bukc_d = din("buk_c", (128, 8))

    tw_d = din("tw", (D, D), I32)
    fbr_d = din("fb_rep", (128, D))
    asc_d = din("ascale", (1, 1))

    wqkv_d = din("wqkv", (D, 3 * D))
    bqk_d = din("bqk_row", (1, 2 * D))
    bv1_d = din("bv1", (1, D))
    wo1_d = din("wo1", (D, D))
    bwo1_d = din("bwo1", (1, D))

    win_d = din("win", (D, 64))
    binc_d = din("bin_c", (64, 1))
    w1c_d = din("w1c", (128, 8))
    b1c_d = din("b1c", (128, 8))
    emat_d = din("emat", (64, D))
    e2_d = din("e2", (D, 64))
    w2c_d = din("w2c", (128, 8))
    browf_d = din("browf", (1, D + 64))
    wout1_d = din("wout1", (64, D))

    ident_d = din("ident", (128, 128))
    cmask_d = din("cmask", (128, 128))

    ctx = ExitStack()
    with ctx:
        E = ctx.enter_context
        X = E(nc.sbuf_tensor("X", [128, NT * D], F32))
        XNT = E(nc.sbuf_tensor("XNT", [128, NCH * S], BF16))
        CTQ = E(nc.sbuf_tensor("CTQ", [128, 8192], BF16))
        CTK = E(nc.sbuf_tensor("CTK", [128, 8192], BF16))
        AO = E(nc.sbuf_tensor("AO", [128, 8192], BF16))
        RB = E(nc.sbuf_tensor("RB", [128, 2048], BF16))
        CT = E(nc.sbuf_tensor("CT", [128, 2048], BF16))
        QF = E(nc.sbuf_tensor("QF", [128, 2048], BF16))
        KF = E(nc.sbuf_tensor("KF", [128, 2048], BF16))
        VP = E(nc.sbuf_tensor("VP", [128, 2080], BF16))
        PB = E(nc.sbuf_tensor("PB", [128, 4096], BF16))
        WSL = E(nc.sbuf_tensor("WSL", [128, 6144], BF16))
        RTS = E(nc.sbuf_tensor("RTS", [128, 2048], BF16))
        SWP = E(nc.sbuf_tensor("SWP", [128, 128], BF16))
        BROW = E(nc.sbuf_tensor("BROW", [1, 6144], BF16))
        F1 = E(nc.sbuf_tensor("F1", [128, 1024], F32))
        STAT = E(nc.sbuf_tensor("STAT", [128, 32], F32))
        ST2 = E(nc.sbuf_tensor("ST2", [128, 272], F32))
        REP = E(nc.sbuf_tensor("REP", [128, 96], F32))
        MB = E(nc.sbuf_tensor("MB", [128, 2], BF16))
        IDE = E(nc.sbuf_tensor("IDE", [128, 128], BF16))
        CMK = E(nc.sbuf_tensor("CMK", [128, 128], BF16))
        ONER = E(nc.sbuf_tensor("ONER", [1, 512], BF16))
        ONEC = E(nc.sbuf_tensor("ONEC", [128, 1], BF16))
        ONERF = E(nc.sbuf_tensor("ONERF", [65, 64], F32))
        CONSTF = E(nc.sbuf_tensor("CONSTF", [128, 2], F32))

        PS = E(nc.psum_tensor("PS", [128, 2048], F32))
        PVA = E(nc.psum_tensor("PVA", [65, 1024], F32))
        PVB = E(nc.psum_tensor("PVB", [65, 1024], F32))

        sem = Sems(ctx, nc, ["T", "V", "Sc", "D1", "D2", "G"])
        res = Res(sem)
        blk = E(nc.Block())

        def T(f):
            blk.tensor(f)

        def V(f):
            blk.vector(f)

        def SC(f):
            blk.scalar(f)

        def D1(f):
            blk.sync(f)

        def D2(f):
            blk.gpsimd(f)

        B_QR, B_UQ, B_KR, B_UK, B_UV = 0, 1024, 2048, 3072, 4096
        B_WO = 5120
        B_V1, B_FO, B_B2 = 0, 1024, 2048
        B_BQK1 = 2048
        R_BA, R_SREP, R_RCP = 0, 1, 2
        R_BDQ, R_BDKV = 4, 8
        R_W1C, R_B1C, R_W2C = 32, 40, 48
        R_BQR, R_BUQ, R_BKR, R_BUK = 64, 72, 80, 88

        PSa = PS[:, 0:1024]
        PSb = PS[:, 1024:2048]

        def wsl(h, off, n):
            return WSL[:, h * 3072 + off:h * 3072 + off + n]

        def evac(preg, dreg, dst, src, use_sc):
            if use_sc:
                scgrp((preg,), (dreg,),
                      lambda sc, dst=dst, src=src: sc.copy(dst, src))
            else:
                vgrp((preg,), (dreg,),
                     lambda v, dst=dst, src=src: v.tensor_copy(dst, src))

        STREAMS = {"T": [], "V": [], "Sc": [], "D1": [], "D2": []}
        ORDER = []

        _EXP = {"PSa": ("PSa0", "PSa1"), "PSb": ("PSb0", "PSb1")}

        def _expand(regions):
            out = []
            for r in regions:
                out.extend(_EXP.get(r, (r,)))
            return out

        def _grp(stream, ename, sname, reads, writes, body, drain=False,
                 inc_n=1):
            reads = _expand(reads)
            writes = _expand(writes)
            pend = []
            res.rd(pend, ename, *reads)
            res.wr(pend, ename, *writes)
            tok = sem.reserve(sname, inc_n)
            STREAMS[stream].append((pend, body, [(sname, inc_n)], drain))
            ORDER.append((stream, STREAMS[stream][-1]))
            res.did_rd(ename, tok, *reads)
            res.did_wr(ename, tok, *writes)

        def tgrp(reads, writes, body):
            _grp("T", "T", "T", reads, writes, body)

        def vgrp(reads, writes, body, drain=False):
            _grp("V", "V", "V", reads, writes, body, drain)

        def scgrp(reads, writes, body, drain=False):
            _grp("Sc", "Sc", "Sc", reads, writes, body, drain)

        def ggrp(reads, writes, body):
            _grp("D2", "G", "G", reads, writes, body)

        def dma(q, dst, src, writes, reads=()):
            _grp(q, q, q, reads, writes,
                 lambda d, dst=dst, src=src: d.dma_start(dst, src),
                 inc_n=16)

        dbg_on = bool(os.environ.get("KDBG"))
        dbg_names = []

        def dbg(name, ap, shape, reads):
            if not dbg_on:
                return
            dt = nc.dram_tensor("dbg_" + name, list(shape),
                                ap.tensor.dtype,
                                kind="ExternalOutput").ap()
            dbg_names.append(name)
            pend = []
            res.rd(pend, "D1", *reads)
            tok = sem.reserve("D1", 16)
            STREAMS["D1"].append((pend,
                                  lambda d, dt=dt, ap=ap: d.dma_start(dt, ap),
                                  [("D1", 16)], False))
            res.did_rd("D1", tok, *reads)

        def flush_streams():
            def run(eng, lst):
                for waits, body, incs, drain in lst:
                    for s, t in waits:
                        eng.wait_ge(sem.h[s], t)
                    last = body(eng)
                    if drain:
                        last = eng.drain()
                    for s, n in incs:
                        last.then_inc(sem.h[s], n)
            sect = {"T": T, "V": V, "Sc": SC, "D1": D1, "D2": D2}
            if os.environ.get("KSECT") == "many":
                for stream, entry in ORDER:
                    sect[stream](lambda e, entry=entry: run(e, [entry]))
                tail = STREAMS["D1"][len([1 for s, _ in ORDER
                                          if s == "D1"]):]
                if tail:
                    D1(lambda d: run(d, tail))
                return
            T(lambda te: run(te, STREAMS["T"]))
            V(lambda v: run(v, STREAMS["V"]))
            SC(lambda sc: run(sc, STREAMS["Sc"]))
            D1(lambda d: run(d, STREAMS["D1"]))
            D2(lambda d: run(d, STREAMS["D2"]))

        dma("D1", X[:, :].rearrange("p (t d) -> p t d", t=NT),
            x_in.rearrange("(t p) d -> p t d", p=128), ("X",))
        dma("D2", IDE[:, :], ident_d, ("IDE",))
        dma("D2", CMK[:, :], cmask_d, ("CMK",))
        RC0 = F1[0:33, 0:1024].bitcast(BF16)
        dma("D2", RC0, rc0_d, ("F1",))
        dma("D2", SWP[:, :], swp_d, ("SWP",))
        dma("D2", REP[:, R_BQR:R_BQR + 8], bqrc_d, ("REP",))
        dma("D2", REP[:, R_BUQ:R_BUQ + 8], buqc_d, ("REP",))
        dma("D2", REP[:, R_BKR:R_BKR + 8], bkrc_d, ("REP",))
        dma("D2", REP[:, R_BUK:R_BUK + 8], bukc_d, ("REP",))
        dma("D2", BROW[:, 0:5 * D], brow0_d, ("BROW",))
        dma("D2", BROW[:, B_WO:B_WO + D], bwo0_d, ("BROW",))
        dma("D2", STAT[0:1, 8:9], asc_d, ("STATf",))
        dma("D2", REP[:, R_BDQ:R_BDQ + 4], bdq_d, ("REP",))
        dma("D2", REP[:, R_BDKV:R_BDKV + 4], bdkv_d, ("REP",))

        vgrp((), ("ONES", "VP", "CONSTF"), lambda v: (
            v.memset(ONER[:, :], 1.0),
            v.memset(ONEC[:, :], 1.0),
            v.memset(ONERF[:, :], 1.0),
            v.memset(CONSTF[:, 0:1], 1e-5),
            v.memset(CONSTF[:, 1:2], -1.0),
            v.memset(VP[:, :], 1.0))[-1])

        LN_MV, LN_STD, LN_RSTD, LN_NMR = 192, 224, 240, 256

        def emit_ln_stats_all():
            def vb(v):
                last = None
                for t in range(NT):
                    v.bn_stats(ST2[:, t * 12:t * 12 + 6],
                               X[:, t * D:t * D + 512])
                    last = v.bn_stats(ST2[:, t * 12 + 6:t * 12 + 12],
                                      X[:, t * D + 512:t * D + 1024])
                return last
            vgrp(("X",), ("BNS",), vb, drain=True)

            def vb2(v):
                last = None
                for t in range(NT):
                    last = v.bn_aggr(
                        ST2[:, LN_MV + 2 * t:LN_MV + 2 * t + 2],
                        ST2[:, t * 12:t * 12 + 12])
                return last
            vgrp(("BNS",), ("LNV",), vb2, drain=True)
            varv = ST2[:, LN_MV:LN_MV + 32].rearrange(
                "p (t k) -> p t k", k=2)[:, :, 1:2]
            stdv = ST2[:, LN_STD:LN_STD + 16].rearrange(
                "p (t k) -> p t k", k=1)
            scgrp(("LNV", "CONSTF"), ("LNV",), lambda sc: (
                sc.activation(stdv, varv, AF.Ln, bias=CONSTF[:, 0:1]),
                sc.drain(),
                sc.activation(ST2[:, LN_RSTD:LN_RSTD + 16],
                              ST2[:, LN_STD:LN_STD + 16], AF.Exp,
                              scale=-0.5))[-1])
            meanv = ST2[:, LN_MV:LN_MV + 32].rearrange(
                "p (t k) -> p t k", k=2)[:, :, 0:1]
            nmrv = ST2[:, LN_NMR:LN_NMR + 16].rearrange(
                "p (t k) -> p t k", k=1)
            vgrp(("LNV",), ("LNV",), lambda v: v.scalar_tensor_tensor(
                nmrv, meanv, -1.0,
                ST2[:, LN_RSTD:LN_RSTD + 16].rearrange(
                    "p (t k) -> p t k", k=1),
                ALU.mult, ALU.mult), drain=True)

        def ln_apply(t, dst, dreg):
            scgrp(("X", "LNV"), (dreg,), lambda sc: sc.activation(
                dst, X[:, t * D:(t + 1) * D], AF.Identity,
                bias=ST2[:, LN_NMR + t:LN_NMR + t + 1],
                scale=ST2[:, LN_RSTD + t:LN_RSTD + t + 1]))

        def emit_ln_to_xnt():
            emit_ln_stats_all()
            for t in range(NT):
                lreg = "RB%d" % (t % 2)
                lnb = RB[:, (t % 2) * D:(t % 2) * D + D]
                ln_apply(t, lnb, lreg)
                for half in range(2):
                    preg = "PSa" if half == 0 else "PSb"
                    pt = (PSa if half == 0 else PSb)[:, 0:256].bitcast(BF16)

                    def tb(te, lnb=lnb, half=half, pt=pt):
                        last = None
                        for cc in range(4):
                            c = half * 4 + cc
                            last = te.transpose(
                                pt[:, cc * 128:cc * 128 + 128],
                                lnb[:, c * 128:c * 128 + 128], IDE[:, :])
                        return last
                    tgrp((lreg, "IDE"), (preg,), tb)

                    def vb(v, t=t, half=half, pt=pt):
                        dst = XNT[:, :].rearrange(
                            "p (c s) -> p c s", c=NCH)[
                            :, half * 4:half * 4 + 4,
                            t * 128:t * 128 + 128]
                        return v.tensor_copy(
                            dst, pt.rearrange("p (c s) -> p c s", c=4))
                    vgrp((preg,), ("XNT",), vb)

        def emit_ct():
            dma("D2", AO[:, 0:4096].rearrange("p (c n) -> p c n", c=NCH),
                wdq_d.rearrange("(c p) n -> p c n", p=128), ("AO",))
            dma("D2", AO[:, 4096:8192].rearrange("p (c n) -> p c n", c=NCH),
                wdkv_d.rearrange("(c p) n -> p c n", p=128), ("AO",))
            for which in range(2):
                wof = 4096 * which
                dst = CTQ if which == 0 else CTK
                dreg = "CTQ" if which == 0 else "CTK"
                bcol = R_BDQ if which == 0 else R_BDKV
                for cs in range(4):
                    for fs in range(4):
                        half = (cs * 4 + fs) % 2
                        preg = "PSa" if half == 0 else "PSb"
                        pslc = (PSa if half == 0 else PSb)[:, 0:512]

                        def tb(te, wof=wof, cs=cs, fs=fs, pslc=pslc):
                            last = None
                            for dc in range(NCH):
                                last = te.matmul(
                                    pslc,
                                    AO[:, wof + dc * 512 + cs * 128:
                                       wof + dc * 512 + cs * 128 + 128],
                                    XNT[:, dc * 2048 + fs * 512:
                                        dc * 2048 + fs * 512 + 512],
                                    start=(dc == 0), stop=(dc == NCH - 1))
                            return last
                        tgrp(("AO", "XNT"), (preg,), tb)
                        scgrp((preg, "REP"), (dreg,),
                              lambda sc, dst=dst, cs=cs, fs=fs, pslc=pslc,
                              bcol=bcol: sc.activation(
                                  dst[:, cs * 2048 + fs * 512:
                                      cs * 2048 + fs * 512 + 512],
                                  pslc, AF.Identity,
                                  bias=REP[:, bcol + cs:bcol + cs + 1]))

        def load_l0_slices(c):
            h = c % 2
            wreg = "WSL%d" % h
            sl = (c * 128, (c + 1) * 128)
            dma("D2", wsl(h, 0, 512).rearrange("p (cc j) -> p cc j", cc=4),
                wqr_d.rearrange("(cc p) d -> p cc d", p=128)[:, :, sl[0]:sl[1]],
                (wreg,))
            dma("D2", wsl(h, 512, 512).rearrange("p (cc j) -> p cc j", cc=4),
                wuq_d.rearrange("(cc p) d -> p cc d", p=128)[:, :, sl[0]:sl[1]],
                (wreg,))
            dma("D2", wsl(h, 1024, 512).rearrange("p (cc j) -> p cc j", cc=4),
                wuk_d.rearrange("(cc p) d -> p cc d", p=128)[:, :, sl[0]:sl[1]],
                (wreg,))
            dma("D2", wsl(h, 1536, 512).rearrange("p (cc j) -> p cc j", cc=4),
                wuv_d.rearrange("(cc p) d -> p cc d", p=128)[:, :, sl[0]:sl[1]],
                (wreg,))
            dma("D2", wsl(h, 2048, 1024).rearrange("p (cc j) -> p cc j", cc=8),
                wkr_d.rearrange("(cc p) d -> p cc d", p=128)[:, :, sl[0]:sl[1]],
                (wreg,))
            dma("D2", RTS[:, :], rts_d[c], ("RTS",))

        def load_l1_slices(c):
            h = c % 2
            wreg = "WSL%d" % h
            wre = wqkv_d.rearrange("(dc p) n -> p dc n", p=128)
            for k, off in ((0, 0), (1, 1024), (2, 2048)):
                dma("D2", wsl(h, k * 1024, 1024).rearrange(
                    "p (dc j) -> p dc j", dc=8),
                    wre[:, :, off + c * 128:off + c * 128 + 128], (wreg,))

        def jit_proj_fm(c, wcol, nchunk, srcT, sreg, bcol, dst, dreg):
            h = c % 2
            wreg = "WSL%d" % h
            for fs in range(4):
                qreg = ("PSa0", "PSb0", "PSa1", "PSb1")[fs]
                pslc = (PSa[:, 0:512], PSb[:, 0:512],
                        PSa[:, 512:1024], PSb[:, 512:1024])[fs]

                def tb(te, fs=fs, pslc=pslc, h=h, wcol=wcol, nchunk=nchunk,
                       srcT=srcT):
                    last = None
                    for cc in range(nchunk):
                        last = te.matmul(
                            pslc,
                            wsl(h, wcol + cc * 128, 128),
                            srcT[:, cc * 2048 + fs * 512:
                                 cc * 2048 + fs * 512 + 512],
                            start=(cc == 0), stop=(cc == nchunk - 1))
                    return last
                tgrp((wreg, sreg), (qreg,), tb)
                if fs % 2 == 0:
                    scgrp((qreg, "REP"), (dreg,), lambda sc, fs=fs,
                          pslc=pslc, bcol=bcol, dst=dst: sc.activation(
                              dst[:, fs * 512:fs * 512 + 512], pslc,
                              AF.Identity,
                              bias=REP[:, bcol + c:bcol + c + 1]))
                else:
                    vgrp((qreg, "REP"), (dreg,), lambda v, fs=fs, pslc=pslc,
                         bcol=bcol, dst=dst: v.tensor_scalar(
                             dst[:, fs * 512:fs * 512 + 512], pslc,
                             REP[:, bcol + c:bcol + c + 1], None, ALU.add))


        def jit_v_l0(c):
            h = c % 2
            wreg = "WSL%d" % h
            for half in range(2):
                preg = "PSa" if half == 0 else "PSb"
                phalf = PSa if half == 0 else PSb

                def tb(te, half=half, phalf=phalf, h=h):
                    last = None
                    for tt in range(8):
                        t = half * 8 + tt
                        o = phalf[:, tt * 128:tt * 128 + 128]
                        last = te.matmul(
                            o, ONER[0:1, 0:128],
                            BROW[0:1, B_UV + c * 128:B_UV + c * 128 + 128],
                            start=True, stop=False)
                        for cc in range(4):
                            last = te.matmul(
                                o,
                                CTK[:, cc * 2048 + t * 128:
                                    cc * 2048 + t * 128 + 128],
                                wsl(h, 1536 + cc * 128, 128),
                                start=False, stop=(cc == 3))
                    return last
                tgrp(("CTK", wreg, "ONES", "BROW"), (preg,), tb)

                def vb(v, half=half, phalf=phalf):
                    dst = VP[:, :].rearrange(
                        "p (kb a u) -> p kb a u", kb=16, a=2)[
                        :, half * 8:half * 8 + 8, :, 0:64]
                    return v.tensor_copy(
                        dst, phalf[:, 0:1024].rearrange(
                            "p (t a s) -> p t a s", t=8, a=2))
                vgrp((preg,), ("VP",), vb)

        def rope_fm(c, dst, dreg):
            def tbs(te):
                last = None
                for fs in range(4):
                    last = te.matmul(PS[:, fs * 512:fs * 512 + 512],
                                     SWP[:, :],
                                     CT[:, fs * 512:fs * 512 + 512],
                                     start=True, stop=True)
                return last
            tgrp(("CTr", "SWP"), ("PSa", "PSb"), tbs)

            def vbr(v):
                v.tensor_tensor(RB[:, :], PS[:, :], RTS[:, :], ALU.mult)
                v.drain()
                v.tensor_tensor(dst, dst, CT[:, :], ALU.add)
                v.drain()
                return v.tensor_tensor(dst, dst, RB[:, :], ALU.add)
            vgrp(("PSa", "PSb", "RTS", "CTr", dreg), 
                 (dreg, "RB0", "RB1"), vbr)

            if c == 0:
                def vb0(v):
                    for r in (0, 32):
                        v.tensor_tensor(RB[r:r + 1, 0:2048],
                                        CT[r:r + 1, :],
                                        RC0[r:r + 1, :], ALU.mult)
                    v.drain()
                    last = None
                    for r in (0, 32):
                        last = v.tensor_tensor(dst[r:r + 1, :],
                                               dst[r:r + 1, :],
                                               RB[r:r + 1, 0:2048], ALU.add)
                    return last
                vgrp(("CTr", "F1", dreg), (dreg, "RB0", "RB1"), vb0)


        def nb_square(which, buf, breg):
            off = which * 2048
            pr = ("PB0", "PB1") if which == 0 else ("PB2", "PB3")
            vgrp((breg,), pr, lambda v, buf=buf, off=off:
                 v.tensor_tensor(PB[:, off:off + 2048], buf[:, :],
                                 buf[:, :], ALU.mult))

        def nb_reduce(which):
            off = which * 2048
            prr = ("PB0", "PB1") if which == 0 else ("PB2", "PB3")
            ph = PSa if which == 0 else PSb
            pr = ("PSa0", "PSa1") if which == 0 else ("PSb0", "PSb1")

            def tb(te, ph=ph, off=off):
                last = None
                for s4 in range(4):
                    last = te.matmul(
                        ph[0:1, (s4 % 2) * 512:(s4 % 2) * 512 + 512],
                        ONEC[:, :],
                        PB[:, off + s4 * 512:off + s4 * 512 + 512],
                        start=(s4 < 2), stop=(s4 >= 2))
                return last
            tgrp(prr + ("ONES",), pr, tb)
            mcol = 17 + which
            vgrp(pr, ("STATn",), lambda v, ph=ph, mcol=mcol:
                 v.reduce_max(STAT[0:1, mcol:mcol + 1], ph[0:1, 0:1024],
                              axis=AX.X), drain=True)

        def nb_combine(scale):
            def vb(v):
                v.tensor_tensor(STAT[0:1, 21:22], STAT[0:1, 17:18],
                                STAT[0:1, 18:19], ALU.add)
                v.drain()
                v.tensor_scalar(STAT[0:1, 21:22], STAT[0:1, 21:22],
                                -0.5 * scale, None, ALU.mult)
                v.drain()
                return v.tensor_copy(MB[0:1, 1:2], STAT[0:1, 21:22])
            vgrp(("STATn",), ("STATn", "MB"), vb)
            tgrp(("MB", "ONES"), ("PSa0",), lambda te:
                 te.matmul(PSa[:, 40:41], ONER[0:1, 0:128], MB[0:1, 1:2],
                           start=True, stop=True))
            vgrp(("PSa0",), ("REP",), lambda v:
                 v.tensor_copy(REP[:, R_BA:R_BA + 1], PSa[:, 40:41]))

        pb_ctr = [0]

        def emit_attention(c, layer):
            scale = L0_SCALE if layer == 0 else L1_SCALE
            causal = (layer == 1)
            aoc = (c % 4) * 2048
            nb_combine(scale)
            for qh in range(2):
                q0 = qh * 1024
                kb_list = list(range(NKB)) if not causal else \
                    list(range(0, qh * 8 + 8))
                subs = []
                for j, kb in enumerate(kb_list):
                    lo = 0 if not causal else max(0, kb * 128 - q0)
                    for qs in range(2):
                        a0 = max(lo, qs * 512)
                        b0 = (qs + 1) * 512
                        if a0 >= b0:
                            continue
                        msk = (causal and kb * 128 >= q0 and
                               qs * 512 <= lo < (qs + 1) * 512)
                        subs.append([kb, qs, a0, b0, j == 0,
                                     j == len(kb_list) - 1, msk, lo])

                def emit_qk_exp(sub):
                    kb, qs, a0, b0, first, last_kb, msk, lo = sub
                    pb = pb_ctr[0] % 4
                    pb_ctr[0] += 1
                    sub.append(pb)
                    pbreg = "PB%d" % pb
                    psq = ("PSa%d" % qs, "PSb%d" % qs)

                    def tb(te, kb=kb, a0=a0, b0=b0, q0=q0):
                        last = None
                        for (rb, psx) in ((0, PSa), (64, PSb)):
                            last = te.matmul(
                                psx[:, a0:b0],
                                KF[rb:rb + 64, kb * 128:kb * 128 + 128],
                                QF[rb:rb + 64, q0 + a0:q0 + b0],
                                start=True, stop=True,
                                tile_position=(rb, 0))
                        return last
                    tgrp(("QF", "KF"), psq, tb)

                    def scb(sc, a0=a0, b0=b0, pb=pb, qs=qs):
                        i_ap = PS[:, :].rearrange(
                            "p (h q) -> p h q", h=2)[:, :, a0:b0]
                        o_ap = PB[:, pb * 1024:pb * 1024 + 1024].rearrange(
                            "p (h q) -> p h q", h=2)[
                            :, :, a0 - qs * 512:b0 - qs * 512]
                        return sc.activation(o_ap, i_ap, AF.Exp,
                                             bias=REP[:, R_BA:R_BA + 1],
                                             scale=scale)
                    scgrp(psq + ("REP",), (pbreg,), scb)

                    if msk:
                        def vb(v, lo=lo, pb=pb, qs=qs):
                            ll = lo - qs * 512
                            pA = PB[:, pb * 1024 + ll:pb * 1024 + ll + 128]
                            pB = PB[:, pb * 1024 + 512 + ll:
                                    pb * 1024 + 512 + ll + 128]
                            v.tensor_tensor(pA, pA, CMK[:, :], ALU.mult)
                            return v.tensor_tensor(pB, pB, CMK[:, :],
                                                   ALU.mult)
                        vgrp((pbreg, "CMK"), (pbreg,), vb)

                def emit_pv(sub):
                    kb, qs, a0, b0, first, last_kb, msk, lo, pb = sub
                    pbreg = "PB%d" % pb

                    def tb2(te, kb=kb, a0=a0, b0=b0, pb=pb, qs=qs,
                            first=first, last_kb=last_kb):
                        last = None
                        for hh, pv in ((0, PVA), (1, PVB)):
                            last = te.matmul(
                                pv[:, a0:b0],
                                VP[:, kb * 130 + hh * 65:
                                   kb * 130 + hh * 65 + 65],
                                PB[:, pb * 1024 + hh * 512 + a0 - qs * 512:
                                   pb * 1024 + hh * 512 + b0 - qs * 512],
                                start=first, stop=last_kb,
                                skip_group_check=True)
                        return last
                    tgrp((pbreg, "VP"), ("PVA", "PVB"), tb2)

                for i, sub in enumerate(subs):
                    emit_qk_exp(sub)
                    if i >= 2:
                        emit_pv(subs[i - 2])
                emit_pv(subs[-2])
                emit_pv(subs[-1])

                rdA = F1[0:1, 0:1024]
                rdB = F1[64:65, 0:1024]
                PBF = PB[:, :].bitcast(F32)

                def scl(sc, rdA=rdA, rdB=rdB):
                    sc.activation(rdA, PVA[64:65, 0:1024], AF.Ln)
                    return sc.activation(rdB, PVB[64:65, 0:1024], AF.Ln)
                scgrp(("PVA", "PVB"), ("F1",), scl)

                def tb3(te, rdA=rdA, rdB=rdB):
                    last = None
                    for fs in range(2):
                        last = te.matmul(
                            PSa[0:64, fs * 512:fs * 512 + 512],
                            ONERF[0:1, :],
                            rdA[:, fs * 512:fs * 512 + 512],
                            start=True, stop=True)
                        last = te.matmul(
                            PSb[0:64, fs * 512:fs * 512 + 512],
                            ONERF[64:65, :],
                            rdB[:, fs * 512:fs * 512 + 512],
                            start=True, stop=True)
                    return last
                tgrp(("F1", "ONES"), ("PSa", "PSb"), tb3)

                scgrp(("PSa", "PSb"), ("PB0", "PB1"), lambda sc, PBF=PBF: (
                    sc.activation(PBF[0:64, 0:1024], PSa[0:64, 0:1024],
                                  AF.Exp, scale=-1.0),
                    sc.activation(PBF[0:64, 1024:2048], PSb[0:64, 0:1024],
                                  AF.Exp, scale=-1.0))[-1])

                def vb2(v, qh=qh, aoc=aoc, PBF=PBF):
                    v.tensor_tensor(AO[0:64, aoc + qh * 1024:
                                       aoc + qh * 1024 + 1024],
                                    PVA[0:64, 0:1024], PBF[0:64, 0:1024],
                                    ALU.mult)
                    return v.tensor_tensor(
                        CT[0:64, qh * 1024:qh * 1024 + 1024],
                        PVB[0:64, 0:1024], PBF[0:64, 1024:2048], ALU.mult)
                vgrp(("PVA", "PVB", "PB0", "PB1"), ("AO", "CTr"), vb2)

            def tb4(te):
                last = None
                for fs in range(4):
                    last = te.matmul(
                        PS[64:128, fs * 512:fs * 512 + 512],
                        IDE[0:64, 0:64],
                        CT[0:64, fs * 512:fs * 512 + 512],
                        start=True, stop=True, tile_position=(0, 64))
                return last
            tgrp(("CTr", "IDE"), ("PSa", "PSb"), tb4)
            vgrp(("PSa", "PSb"), ("AO",), lambda v, aoc=aoc:
                 v.tensor_copy(AO[64:128, aoc:aoc + 2048], PS[64:128, :]))

        def emit_l0_pair(c):
            if c + 1 < NP:
                load_l0_slices(c + 1)
            elif BUILD >= 3:
                load_l1_slices(0)
            jit_proj_fm(c, 512, 4, CTQ, "CTQ", R_BUQ, QF, "QF")
            jit_proj_fm(c, 0, 4, CTQ, "CTQ", R_BQR, CT, "CTr")
            rope_fm(c, QF[:, :], "QF")
            nb_square(0, QF, "QF")
            jit_proj_fm(c, 1024, 4, CTK, "CTK", R_BUK, KF, "KF")
            jit_proj_fm(c, 2048, 8, XNT, "XNT", R_BKR, CT, "CTr")
            rope_fm(c, KF[:, :], "KF")
            nb_square(1, KF, "KF")
            jit_v_l0(c)
            nb_reduce(0)
            nb_reduce(1)
            emit_attention(c, 0)

        def emit_l1_pair(c):
            if c + 1 < NP:
                load_l1_slices(c + 1)
            h = c % 2
            wreg = "WSL%d" % h
            for which, dst, dreg in ((0, QF, "QF"), (1, KF, "KF")):
                def emit_qk(which=which, dst=dst, dreg=dreg):
                    for fs in range(4):
                        half = fs % 2
                        preg = "PSa" if half == 0 else "PSb"
                        pslc = (PSa if half == 0 else PSb)[:, 0:512]

                        def tb2(te, which=which, fs=fs, pslc=pslc, h=h):
                            last = None
                            for dc in range(NCH):
                                last = te.matmul(
                                    pslc,
                                    wsl(h, which * 1024 + dc * 128, 128),
                                    XNT[:, dc * 2048 + fs * 512:
                                        dc * 2048 + fs * 512 + 512],
                                    start=(dc == 0), stop=False)
                            return te.matmul(
                                pslc,
                                BROW[0:1, B_BQK1 + which * 1024 + c * 128:
                                     B_BQK1 + which * 1024 + c * 128 + 128],
                                ONER[0:1, 0:512],
                                start=False, stop=True)
                        tgrp((wreg, "XNT", "BROW", "ONES"), (preg,), tb2)
                        evac(preg, dreg, dst[:, fs * 512:fs * 512 + 512],
                             pslc, fs % 2 == 0)
                emit_qk()
                nb_square(which, dst, dreg)
            for half in range(2):
                preg = "PSa" if half == 0 else "PSb"
                phalf = PSa if half == 0 else PSb

                def tb(te, half=half, phalf=phalf, h=h):
                    last = None
                    for tt in range(8):
                        t = half * 8 + tt
                        o = phalf[:, tt * 128:tt * 128 + 128]
                        last = te.matmul(
                            o, ONER[0:1, 0:128],
                            BROW[0:1, B_V1 + c * 128:B_V1 + c * 128 + 128],
                            start=True, stop=False)
                        for dc in range(NCH):
                            last = te.matmul(
                                o,
                                XNT[:, dc * 2048 + t * 128:
                                    dc * 2048 + t * 128 + 128],
                                wsl(h, 2048 + dc * 128, 128),
                                start=False, stop=(dc == NCH - 1))
                    return last
                tgrp(("XNT", wreg, "ONES", "BROW"), (preg,), tb)

                def vb(v, half=half, phalf=phalf):
                    dst = VP[:, :].rearrange(
                        "p (kb a u) -> p kb a u", kb=16, a=2)[
                        :, half * 8:half * 8 + 8, :, 0:64]
                    return v.tensor_copy(
                        dst, phalf[:, 0:1024].rearrange(
                            "p (t a s) -> p t a s", t=8, a=2))
                vgrp((preg,), ("VP",), vb)
            nb_reduce(0)
            nb_reduce(1)
            emit_attention(c, 1)

        def emit_wo(layer, halfgrp):
            wo_d = wo0_d if layer == 0 else wo1_d
            wor = wo_d.rearrange("(ch p) n -> p ch n", p=128)
            for c4 in range(4):
                ch = halfgrp * 4 + c4
                wreg = "PB%d" % c4
                dma("D2", PB[:, c4 * 1024:c4 * 1024 + 1024],
                    wor[:, ch, :], (wreg,))
            for t in range(NT):
                half = t % 2
                preg = "PSa" if half == 0 else "PSb"
                phalf = PSa if half == 0 else PSb

                def tb(te, t=t, phalf=phalf, halfgrp=halfgrp):
                    last = None
                    if halfgrp == 1:
                        for fs in range(2):
                            last = te.matmul(
                                phalf[:, fs * 512:fs * 512 + 512],
                                ONER[0:1, 0:128],
                                BROW[0:1, B_WO + fs * 512:
                                     B_WO + fs * 512 + 512],
                                start=True, stop=False)
                    for c4 in range(4):
                        for fs in range(2):
                            last = te.matmul(
                                phalf[:, fs * 512:fs * 512 + 512],
                                AO[:, c4 * 2048 + t * 128:
                                   c4 * 2048 + t * 128 + 128],
                                PB[:, c4 * 1024 + fs * 512:
                                   c4 * 1024 + fs * 512 + 512],
                                start=(c4 == 0 and halfgrp == 0),
                                stop=(c4 == 3))
                    return last
                tgrp(("AO", "PB0", "PB1", "PB2", "PB3", "ONES", "BROW"),
                     (preg,), tb)
                vgrp((preg, "X"), ("X",), lambda v, t=t, phalf=phalf:
                     v.tensor_tensor(X[:, t * D:(t + 1) * D],
                                     X[:, t * D:(t + 1) * D],
                                     phalf[:, 0:1024], ALU.add))

        def emit_l0_ffn():
            XINT = CTK[:, 0:2048].bitcast(I32)
            F2 = CTK[:, 2048:4096].bitcast(F32)
            F3 = CTK[:, 4096:6144].bitcast(F32)
            TSTG = CTK[:, 6144:8192].bitcast(I32)
            WT = CTQ[:, :]
            FBR = QF[:, 0:2048].bitcast(F32)

            dma("D2", FBR, fbr_d, ("QF",))
            def vb(v):
                v.tensor_scalar_max(STAT[0:1, 22:23], STAT[0:1, 8:9], 1e-5)
                v.drain()
                v.reciprocal(STAT[0:1, 23:24], STAT[0:1, 22:23])
                v.drain()
                v.tensor_copy(MB[0:1, 0:1], STAT[0:1, 22:23])
                return v.tensor_copy(MB[0:1, 1:2], STAT[0:1, 23:24])
            vgrp(("STATf",), ("STATf", "MB"), vb)
            def vb2(v):
                v.tensor_tensor(STAT[0:1, 24:25], STAT[0:1, 22:23],
                                MB[0:1, 0:1], ALU.subtract)
                v.tensor_tensor(STAT[0:1, 25:26], STAT[0:1, 23:24],
                                MB[0:1, 1:2], ALU.subtract)
                v.drain()
                v.tensor_copy(PB[0:1, 0:1], STAT[0:1, 24:25])
                return v.tensor_copy(PB[0:1, 1:2], STAT[0:1, 25:26])
            vgrp(("STATf", "MB"), ("STATf", "PB0"), vb2)

            def tb(te):
                te.matmul(PSa[:, 0:1], ONER[0:1, 0:128], MB[0:1, 0:1],
                          start=True, stop=False)
                te.matmul(PSa[:, 0:1], ONER[0:1, 0:128], PB[0:1, 0:1],
                          start=False, stop=True)
                te.matmul(PSa[:, 1:2], ONER[0:1, 0:128], MB[0:1, 1:2],
                          start=True, stop=False)
                return te.matmul(PSa[:, 1:2], ONER[0:1, 0:128], PB[0:1, 1:2],
                                 start=False, stop=True)
            tgrp(("MB", "PB0", "ONES"), ("PSa",), tb)
            vgrp(("PSa",), ("REP",), lambda v: (
                v.tensor_copy(REP[:, R_SREP:R_SREP + 1], PSa[:, 0:1]),
                v.tensor_copy(REP[:, R_RCP:R_RCP + 1], PSa[:, 1:2]))[-1])

            twr = tw_d.rearrange("(ob p) i -> p ob i", p=128)
            for ob in range(NCH):
                dma("D1", TSTG, twr[:, ob, :], ("CTK",))
                scgrp(("CTK", "CONSTF"), ("PB0",), lambda sc:
                      sc.activation(PB[:, 0:1024], TSTG, AF.Identity,
                                    bias=CONSTF[:, 1:2]))
                for g4 in range(2):
                    preg = "PSa" if g4 == 0 else "PSb"
                    pt = (PSa if g4 == 0 else PSb)[:, 0:256].bitcast(BF16)

                    def tb2(te, g4=g4, pt=pt):
                        last = None
                        for j in range(4):
                            ci = g4 * 4 + j
                            last = te.transpose(
                                pt[:, j * 128:j * 128 + 128],
                                PB[:, ci * 128:ci * 128 + 128], IDE[:, :])
                        return last
                    tgrp(("PB0", "IDE"), (preg,), tb2)

                    def vb3(v, ob=ob, g4=g4, pt=pt):
                        dst = WT.rearrange("p (ci u) -> p ci u", ci=NCH)[
                            :, g4 * 4:g4 * 4 + 4, ob * 128:ob * 128 + 128]
                        return v.tensor_copy(
                            dst, pt.rearrange("p (ci u) -> p ci u", ci=4))
                    vgrp((preg,), ("CTQ",), vb3)

            dbg("ffnrep", REP[:, :], (128, 64), ("REP",))
            dbg("ffnwt", CTQ[:, :], (128, 8192), ("CTQ",))
            emit_ln_stats_all()
            for t in range(NT):
                ln_apply(t, F1[:, :], "F1")

                def vb4(v, t=t):
                    v.tensor_scalar(F2[:, 0:1024], F1[:, :],
                                    REP[:, R_RCP:R_RCP + 1], 127.0,
                                    ALU.mult, ALU.min)
                    v.drain()
                    v.tensor_scalar_max(F3[:, 0:1024], F2[:, 0:1024], -127.0)
                    v.drain()
                    v.tensor_copy(XINT[:, 0:1024], F3[:, 0:1024])
                    v.drain()
                    lreg = RB[:, (t % 2) * D:(t % 2) * D + D]
                    return v.tensor_copy(lreg, XINT[:, 0:1024])
                vgrp(("CTK", "F1", "REP"), ("CTK", "RB%d" % (t % 2)), vb4)
                for half in range(2):
                    preg = "PSa" if half == 0 else "PSb"
                    pt = (PSa if half == 0 else PSb)[:, 0:256].bitcast(BF16)

                    def tb3(te, t=t, half=half, pt=pt):
                        last = None
                        for cc in range(4):
                            c = half * 4 + cc
                            last = te.transpose(
                                pt[:, cc * 128:cc * 128 + 128],
                                RB[:, (t % 2) * D + c * 128:
                                   (t % 2) * D + c * 128 + 128], IDE[:, :])
                        return last
                    tgrp(("RB%d" % (t % 2), "IDE"), (preg,), tb3)

                    def vb5(v, t=t, half=half, pt=pt):
                        dst = XNT[:, :].rearrange("p (c s) -> p c s", c=NCH)[
                            :, half * 4:half * 4 + 4, t * 128:t * 128 + 128]
                        return v.tensor_copy(
                            dst, pt.rearrange("p (c s) -> p c s", c=4))
                    vgrp((preg,), ("XNT",), vb5)
                half = t % 2
                preg = "PSa" if half == 0 else "PSb"
                phalf = PSa if half == 0 else PSb

                def tb4(te, t=t, phalf=phalf):
                    last = None
                    for fs in range(2):
                        for ci in range(NCH):
                            last = te.matmul(
                                phalf[:, fs * 512:fs * 512 + 512],
                                XNT[:, ci * 2048 + t * 128:
                                    ci * 2048 + t * 128 + 128],
                                WT[:, ci * 1024 + fs * 512:
                                   ci * 1024 + fs * 512 + 512],
                                start=(ci == 0), stop=(ci == NCH - 1))
                    return last
                tgrp(("XNT", "CTQ"), (preg,), tb4)

                def vb6(v, t=t, phalf=phalf):
                    v.scalar_tensor_tensor(F1[:, :], phalf[:, 0:1024],
                                           REP[:, R_SREP:R_SREP + 1],
                                           FBR[:, 0:1024],
                                           ALU.mult, ALU.add)
                    v.drain()
                    return v.tensor_tensor(X[:, t * D:(t + 1) * D],
                                           X[:, t * D:(t + 1) * D],
                                           F1[:, :], ALU.add)
                vgrp((preg, "REP", "QF", "X", "F1"), ("F1", "X"), vb6)
                if t == 0:
                    dbg("ffnxq", XNT[:, :], (128, NCH * S), ("XNT",))
                    dbg("ffnq0", RB[:, :], (128, 2048), ("RB0", "RB1"))

        def emit_l1_ffn():
            ST = CTQ[0:64, 0:2048]
            EMS = CTQ[0:64, 2048:3072]
            W2E = CTQ[:, 3072:3584]
            WINS = CTQ[:, 3584:4096]
            WOUTS = CTQ[0:64, 4096:5120]
            GT = CTQ[0:64, 5120:7168]
            GTOK = CTQ[:, 7168:8192]
            E2S = CTK[:, 0:512]

            dma("D2", WINS.rearrange("p (dc m) -> p dc m", dc=NCH),
                win_d.rearrange("(dc p) m -> p dc m", p=128), ("CTQ",))
            dma("D2", EMS, emat_d, ("CTQ",))
            dma("D2", E2S.rearrange("p (ci m) -> p ci m", ci=NCH),
                e2_d.rearrange("(ci p) m -> p ci m", p=128), ("CTK",))
            dma("D2", STAT[0:64, 28:29], binc_d, ("STATb",))
            dma("D2", REP[:, R_W1C:R_W1C + 8], w1c_d, ("REP",))
            dma("D2", REP[:, R_B1C:R_B1C + 8], b1c_d, ("REP",))
            dma("D2", REP[:, R_W2C:R_W2C + 8], w2c_d, ("REP",))
            dma("D2", BROW[:, B_FO:B_FO + D + 64], browf_d, ("BROW",))
            dma("D2", WOUTS, wout1_d, ("CTQ",))

            def vb(v):
                last = None
                for ci in range(NCH):
                    last = v.tensor_scalar(
                        W2E[:, ci * 64:ci * 64 + 64],
                        E2S[:, ci * 64:ci * 64 + 64],
                        REP[:, R_W2C + ci:R_W2C + ci + 1], None, ALU.mult)
                return last
            vgrp(("CTK", "REP"), ("CTQ",), vb)

            emit_ln_to_xnt()

            for fs in range(4):
                half = fs % 2
                preg = "PSa" if half == 0 else "PSb"
                pslc = (PSa if half == 0 else PSb)[0:64, 0:512]

                def tb(te, fs=fs, pslc=pslc):
                    last = None
                    for dc in range(NCH):
                        last = te.matmul(
                            pslc, WINS[:, dc * 64:dc * 64 + 64],
                            XNT[:, dc * 2048 + fs * 512:
                                dc * 2048 + fs * 512 + 512],
                            start=(dc == 0), stop=(dc == NCH - 1))
                    return last
                tgrp(("CTQ", "XNT"), (preg,), tb)
                scgrp((preg, "STATb"), ("CTQ",), lambda sc, fs=fs, pslc=pslc:
                      sc.activation(ST[:, fs * 512:fs * 512 + 512], pslc,
                                    AF.Identity, bias=STAT[0:64, 28:29]))

            for ci in range(NCH):
                def tb2(te, ci=ci):
                    last = None
                    for fs in range(4):
                        last = te.matmul(
                            PS[:, fs * 512:fs * 512 + 512],
                            EMS[:, ci * 128:ci * 128 + 128],
                            ST[:, fs * 512:fs * 512 + 512],
                            start=True, stop=True)
                    return last
                tgrp(("CTQ",), ("PSa", "PSb"), tb2)
                scgrp(("PSa", "PSb", "REP"), ("XNT",), lambda sc, ci=ci:
                      sc.activation(XNT[:, ci * 2048:ci * 2048 + 2048],
                                    PS[:, :], AF.Relu,
                                    scale=REP[:, R_W1C + ci:R_W1C + ci + 1],
                                    bias=REP[:, R_B1C + ci:R_B1C + ci + 1]))

            for half in range(2):
                preg = "PSa" if half == 0 else "PSb"
                phalf = PSa if half == 0 else PSb

                def tb3(te, half=half, phalf=phalf):
                    last = None
                    for tt in range(8):
                        t = half * 8 + tt
                        o = phalf[:, tt * 64:tt * 64 + 64]
                        last = te.matmul(o, ONER[0:1, 0:128],
                                         BROW[0:1, B_B2:B_B2 + 64],
                                         start=True, stop=False)
                        for ci in range(NCH):
                            last = te.matmul(
                                o,
                                XNT[:, ci * 2048 + t * 128:
                                    ci * 2048 + t * 128 + 128],
                                W2E[:, ci * 64:ci * 64 + 64],
                                start=False, stop=(ci == NCH - 1))
                    return last
                tgrp(("XNT", "CTQ", "ONES", "BROW"), (preg,), tb3)
                vgrp((preg,), ("CTQ",), lambda v, half=half, phalf=phalf:
                     v.tensor_copy(GTOK[:, half * 512:half * 512 + 512],
                                   phalf[:, 0:512]))
            for g4 in range(4):
                half = g4 % 2
                preg = "PSa" if half == 0 else "PSb"
                pt = (PSa if half == 0 else PSb)[0:64, 0:256].bitcast(BF16)

                def tb4(te, g4=g4, pt=pt):
                    last = None
                    for j in range(4):
                        t = g4 * 4 + j
                        last = te.transpose(pt[:, j * 128:j * 128 + 128],
                                            GTOK[:, t * 64:t * 64 + 64],
                                            IDE[:, :])
                    return last
                tgrp(("CTQ", "IDE"), (preg,), tb4)
                vgrp((preg,), ("CTQ",), lambda v, g4=g4, pt=pt:
                     v.tensor_copy(GT[:, g4 * 512:g4 * 512 + 512], pt))

            for t in range(NT):
                half = t % 2
                preg = "PSa" if half == 0 else "PSb"
                phalf = PSa if half == 0 else PSb

                def tb5(te, t=t, phalf=phalf):
                    last = None
                    for fs in range(2):
                        o = phalf[:, fs * 512:fs * 512 + 512]
                        last = te.matmul(o, ONER[0:1, 0:128],
                                         BROW[0:1, B_FO + fs * 512:
                                              B_FO + fs * 512 + 512],
                                         start=True, stop=False)
                        last = te.matmul(
                            o, GT[:, t * 128:t * 128 + 128],
                            WOUTS[:, fs * 512:fs * 512 + 512],
                            start=False, stop=True)
                    return last
                tgrp(("CTQ", "ONES", "BROW"), (preg,), tb5)
                vgrp((preg, "X"), ("X",), lambda v, t=t, phalf=phalf:
                     v.tensor_tensor(X[:, t * D:(t + 1) * D],
                                     X[:, t * D:(t + 1) * D],
                                     phalf[:, 0:1024], ALU.add))

        load_l0_slices(0)
        emit_ln_to_xnt()
        dbg("xnt", XNT[:, :], (128, NCH * S), ("XNT",))
        emit_ct()
        dbg("ctq", CTQ[:, :], (128, 8192), ("CTQ",))
        dbg("ctk", CTK[:, :], (128, 8192), ("CTK",))
        for c in range(NP):
            emit_l0_pair(c)
            if c == 0 and dbg_on:
                dbg("qf0", QF[:, :], (128, 2048), ("QF",))
                dbg("kf0", KF[:, :], (128, 2048), ("KF",))
                dbg("vp0", VP[:, :], (128, 2080), ("VP",))
                dbg("rep0", REP[:, :], (128, 64), ("REP",))
                dbg("f1r", F1[:, :], (128, 1024), ("F1",))
                dbg("ao0", AO[:, 0:2048], (128, 2048), ("AO",))
            if c == 3:
                if dbg_on:
                    dbg("aoh0", AO[:, :], (128, 8192), ("AO",))
                emit_wo(0, 0)
            if c == 7 and dbg_on:
                dbg("aoh1", AO[:, :], (128, 8192), ("AO",))
        emit_wo(0, 1)
        dbg("xfin", X[:, :], (128, NT * D), ("X",))

        if BUILD >= 2:
            emit_l0_ffn()

        if BUILD >= 3:
            dma("D2", BROW[:, B_V1:B_V1 + D], bv1_d, ("BROW",))
            dma("D2", BROW[:, B_WO:B_WO + D], bwo1_d, ("BROW",))
            dma("D2", BROW[:, B_BQK1:B_BQK1 + 2 * D], bqk_d, ("BROW",))
            emit_ln_to_xnt()
            for c in range(NP):
                emit_l1_pair(c)
                if c == 3:
                    emit_wo(1, 0)
            emit_wo(1, 1)

        if BUILD >= 4:
            emit_l1_ffn()

        pend = []
        res.rd(pend, "D1", "X")
        final_tok = sem.reserve("D1", 16)

        def final_store(d1, pend=pend, final_tok=final_tok):
            for s, t in pend:
                d1.wait_ge(sem.h[s], t)
            i = d1.dma_start(out_d.rearrange("(t p) d -> p t d", p=128),
                             X[:, :].rearrange("p (t d) -> p t d", t=NT))
            i.then_inc(sem.h["D1"], 16)
            d1.wait_ge(sem.h["D1"], final_tok)
            return i
        STREAMS["D1"].append(([], final_store, [], False))

        flush_streams()

    return nc


_NC_CACHE = {}
DBG_OUT = {}


def _theta():
    ids = np.arange(512, dtype=np.float64)
    return (10000.0 ** (-2.0 * ids) / 512.0)


def _perm():
    p = []
    for h in range(16):
        p.extend(range(h * 64, h * 64 + 64, 2))
        p.extend(range(h * 64 + 1, h * 64 + 64, 2))
    return np.array(p)


def kernel(L0_attn_x, L0_attn_Wd_q_w, L0_attn_Wd_q_b, L0_attn_Wu_q_w,
           L0_attn_Wu_q_b, L0_attn_Wq_r_w, L0_attn_Wq_r_b, L0_attn_Wk_r_w,
           L0_attn_Wk_r_b, L0_attn_Wd_kv_w, L0_attn_Wd_kv_b, L0_attn_Wu_k_w,
           L0_attn_Wu_k_b, L0_attn_Wu_v_w, L0_attn_Wu_v_b, L0_attn_Wo_w,
           L0_attn_Wo_b, L0_ffn_ternary_weight, L0_ffn_bias, L0_ffn_act_scale,
           L1_attn_W_qkv, L1_attn_b_qkv, L1_attn_W_out, L1_attn_b_out,
           L1_ffn_Win, L1_ffn_bin_, L1_ffn_W1, L1_ffn_b1, L1_ffn_W2,
           L1_ffn_b2, L1_ffn_Wout, L1_ffn_bout):
    f32 = np.float32
    perm = _perm()

    def c(a):
        return np.ascontiguousarray(np.asarray(a), dtype=f32)

    theta = _theta()
    pos = np.arange(S, dtype=np.float64)
    rts = np.empty((NP, 128, S), dtype=BF)
    r_idx = np.arange(128)
    hd, kk, ii = r_idx // 64, (r_idx % 64) // 32, r_idx % 32
    sign = np.where(kk == 0, -1.0, 1.0)
    for cc in range(NP):
        th = theta[64 * cc + hd * 32 + ii]
        rts[cc] = (sign[:, None] *
                   np.sin(pos[None, :] * th[:, None])).astype(BF)
    rc0 = np.zeros((33, S), dtype=f32)
    c0m1 = np.cos(pos * theta[0]) - 1.0
    rc0[0, :] = c0m1
    rc0[32, :] = c0m1
    swp = np.zeros((128, 128), dtype=f32)
    jj = np.where((r_idx % 64) < 32, r_idx + 32, r_idx - 32)
    swp[r_idx, jj] = 1.0

    bqr = c(L0_attn_Wq_r_b)[perm]
    buq = c(L0_attn_Wu_q_b)[perm]
    bkr = c(L0_attn_Wk_r_b)[perm]
    buk = c(L0_attn_Wu_k_b)[perm]
    brow0 = np.concatenate([bqr, buq, bkr, buk, c(L0_attn_Wu_v_b)])[None, :]

    b_qkv = c(L1_attn_b_qkv)

    common = dict(
        wdq=c(L0_attn_Wd_q_w),
        wdkv=c(L0_attn_Wd_kv_w),
        wuq=np.ascontiguousarray(c(L0_attn_Wu_q_w)[:, perm]),
        wqr=np.ascontiguousarray(c(L0_attn_Wq_r_w)[:, perm]),
        wuk=np.ascontiguousarray(c(L0_attn_Wu_k_w)[:, perm]),
        wkr=np.ascontiguousarray(c(L0_attn_Wk_r_w)[:, perm]),
        wuv=c(L0_attn_Wu_v_w),
        wo0=c(L0_attn_Wo_w),
        bdq_t=np.ascontiguousarray(c(L0_attn_Wd_q_b).reshape(4, 128).T),
        bdkv_t=np.ascontiguousarray(c(L0_attn_Wd_kv_b).reshape(4, 128).T),
        brow0=np.ascontiguousarray(brow0),
        bwo0=c(L0_attn_Wo_b)[None, :],
        rts=rts,
        rc0=rc0,
        swp=swp,
        bqr_c=np.ascontiguousarray(bqr.reshape(8, 128).T),
        buq_c=np.ascontiguousarray(buq.reshape(8, 128).T),
        bkr_c=np.ascontiguousarray(bkr.reshape(8, 128).T),
        buk_c=np.ascontiguousarray(buk.reshape(8, 128).T),
        tw=np.ascontiguousarray(np.asarray(L0_ffn_ternary_weight,
                                           dtype=np.int32)),
        fb_rep=np.ascontiguousarray(np.broadcast_to(c(L0_ffn_bias)[None, :],
                                                    (128, D))),
        ascale=c(L0_ffn_act_scale).reshape(1, 1),
        wqkv=c(L1_attn_W_qkv),
        bqk_row=np.ascontiguousarray(b_qkv[0:2048][None, :]),
        bv1=np.ascontiguousarray(b_qkv[2048:3072][None, :]),
        wo1=c(L1_attn_W_out),
        bwo1=c(L1_attn_b_out)[None, :],
        win=c(L1_ffn_Win),
        bin_c=c(L1_ffn_bin_).reshape(64, 1),
        w1c=np.ascontiguousarray(c(L1_ffn_W1).reshape(1024).reshape(8, 128).T),
        b1c=np.ascontiguousarray(c(L1_ffn_b1).reshape(1024).reshape(8, 128).T),
        emat=np.ascontiguousarray(
            (np.arange(D)[None, :] // 16 == np.arange(64)[:, None])
            .astype(f32)),
        e2=np.ascontiguousarray(
            (np.arange(D)[:, None] // 16 == np.arange(64)[None, :])
            .astype(f32)),
        w2c=np.ascontiguousarray(c(L1_ffn_W2).reshape(1024).reshape(8, 128).T),
        browf=np.concatenate([c(L1_ffn_bout), c(L1_ffn_b2)])[None, :],
        wout1=c(L1_ffn_Wout),
        ident=np.eye(128, dtype=f32),
        cmask=np.ascontiguousarray(
            (np.arange(128)[None, :] >= np.arange(128)[:, None])
            .astype(f32)),
    )

    if "nc" not in _NC_CACHE:
        _NC_CACHE["nc"] = build_program()
    nc = _NC_CACHE["nc"]

    x = np.asarray(L0_attn_x, dtype=f32)
    in_maps = []
    for i in range(8):
        m = dict(common)
        m["x"] = np.ascontiguousarray(x[i])
        in_maps.append(m)
    r = run_bass_kernel_spmd(nc, in_maps, core_ids=list(range(8)))
    if os.environ.get("KDBG"):
        DBG_OUT.clear()
        DBG_OUT.update({k: np.asarray(v) for k, v in r.results[0].items()
                        if k.startswith("dbg_")})
    return np.stack([np.asarray(rr["out"]) for rr in r.results], axis=0)
